# version 09
# baseline (speedup 1.0000x reference)
import math
from contextlib import ExitStack

import numpy as np

import concourse.bass as bass
import concourse.mybir as mybir
from concourse.bass_utils import run_bass_kernel_spmd

F32 = mybir.dt.float32
BF16 = mybir.dt.bfloat16
I8 = mybir.dt.int8
AF = mybir.ActivationFunctionType
ALU = mybir.AluOpType

NHEADS = 32
H = 4096
HD = 128
INTER = 14336
B = 2
S = 4096
R = 1024
W = 4096
QBASE = W - R
NSLOT = (QBASE // 128 + 4, QBASE // 128 + 8)
EPS = 1e-5
QK_SCALE = 1.0 / math.sqrt(HD)
SHIFT = -45.0
NEGBIG = -1e30
HC = 32
OT_G = INTER // 128
OT_D = H // 128

TAPS = ()
ALIBI_CUT = 110.0


def _slot_start(slopes, h, qt):
    ns = NSLOT[qt]
    q0c = QBASE + qt * 512
    for j in range(ns):
        if j >= ns - 4 or slopes[h] * (q0c - 128 * j - 127) < ALIBI_CUT:
            return j
    return ns - 4


def _alibi_slopes(n):
    cp2 = 2 ** math.floor(math.log2(n))
    base = 2.0 ** (-(2.0 ** (-(math.log2(cp2) - 3))))
    slopes = base ** np.arange(1, cp2 + 1, dtype=np.float64)
    if cp2 != n:
        extra_base = 2.0 ** (-(2.0 ** (-(math.log2(2 * cp2) - 3))))
        rem = min(cp2, n - cp2)
        extra = extra_base ** np.arange(1, 1 + 2 * rem, 2, dtype=np.float64)
        slopes = np.concatenate([slopes, extra])
    return slopes




class Ctr:

    def __init__(self, nc, ctx, names):
        self.nc = nc
        self.sems = {n: ctx.enter_context(nc.semaphore(n)) for n in names}
        self.cnt = {n: 0 for n in names}

    def inc(self, ins, name, by=1):
        ins.then_inc(self.sems[name], by)
        self.cnt[name] += by
        return self.cnt[name]

    def c(self, name):
        return self.cnt[name]

    def wait(self, eng, name, val):
        if val > 0:
            eng.wait_ge(self.sems[name], val)


def build_program(taps=()):
    nc = bass.Bass("TRN2", target_bir_lowering=False)

    d_xkv = nc.dram_tensor("xkv", [H, W], F32, kind="ExternalInput")
    d_wq = nc.dram_tensor("wqt", [NHEADS, 128, HC * 128], F32, kind="ExternalInput")
    d_wkv = nc.dram_tensor("wkvt", [128, HC * 256], F32, kind="ExternalInput")
    d_wo = nc.dram_tensor("wot", [OT_D, 128, HC * 128], F32, kind="ExternalInput")
    d_gw = nc.dram_tensor("gwt", [OT_G, 128, HC * 128], I8, kind="ExternalInput")
    d_uw = nc.dram_tensor("uwt", [OT_G, 128, HC * 128], I8, kind="ExternalInput")
    d_dw = nc.dram_tensor("dwt", [OT_D, 128, OT_G * 128], I8, kind="ExternalInput")
    d_scg = nc.dram_tensor("scg", [128, OT_G], F32, kind="ExternalInput")
    d_scu = nc.dram_tensor("scu", [128, OT_G], F32, kind="ExternalInput")
    d_scd = nc.dram_tensor("scd", [128, OT_D], F32, kind="ExternalInput")
    d_dt = nc.dram_tensor("dtiles", [NHEADS, 5, 128, 512], F32, kind="ExternalInput")
    d_bias = nc.dram_tensor("biasv", [128, 2 * NHEADS * 32], F32, kind="ExternalInput")

    d_out = nc.dram_tensor("out", [OT_D, 128, R], F32, kind="ExternalOutput")
    x1_kind = "ExternalOutput" if "x1" in taps else "Internal"
    h_kind = "ExternalOutput" if "h" in taps else "Internal"
    d_x1 = nc.dram_tensor("x1s", [HC, 128, R], F32, kind=x1_kind)
    d_h0 = nc.dram_tensor("h0s", [128, OT_G * 512], BF16, kind=h_kind)
    d_h1 = nc.dram_tensor("h1s", [128, OT_G * 512], BF16, kind=h_kind)
    d_hs = (d_h0, d_h1)
    d_taps = {}
    for t, shape, dt_ in (("kt", [128, W], BF16), ("v", [128, W], BF16),
                          ("qt", [128, NHEADS * R], BF16),
                          ("at", [128, NHEADS * R], BF16),
                          ("xl1", [128, HC * R], BF16),
                          ("rows", [1, 8 * 512], F32)):
        if t in taps:
            d_taps[t] = nc.dram_tensor("tap_" + t, shape, dt_, kind="ExternalOutput")

    names = ["xf0", "xf1", "xf2", "xf3", "st0", "st1x", "one",
             "xb0", "xb1", "wt0", "wt1", "dd0", "dd1", "oneg",
             "sqg", "mm1g", "mm1", "st1", "rsd", "st2", "bcd",
             "dv", "kvc", "kvx", "qmm", "qev",
             "sc", "adv", "ex", "pv", "edv", "epe", "ecp", "ett",
             "omm", "oev",
             "q5", "q4s", "m5", "r5a", "r5b", "r5c", "b5", "n5",
             "mmg", "evg", "hdv", "mmd", "evd", "ydv", "fin"]

    with ExitStack() as top:
        p = Ctr(nc, top, names)
        sb = lambda n, s, d: top.enter_context(nc.sbuf_tensor(n, s, d))

        SY_SEMS = ("xf0", "xf1", "xf2", "xf3", "st0", "st1x", "one")
        GP_SEMS = ("xb0", "xb1", "wt0", "wt1", "dd0", "dd1", "oneg")

        def drain_dma():
            for n_ in SY_SEMS:
                if p.c(n_):
                    nc.sync.wait_ge(p.sems[n_], p.c(n_))
            for n_ in GP_SEMS:
                if p.c(n_):
                    nc.gpsimd.wait_ge(p.sems[n_], p.c(n_))

        s_scg = sb("s_scg", [128, OT_G], F32)
        s_scu = sb("s_scu", [128, OT_G], F32)
        s_scd = sb("s_scd", [128, OT_D], F32)
        s_ones = sb("s_ones", [128, 1], BF16)
        s_ones1 = sb("s_ones1", [1, 128], F32)
        s_rows = sb("s_rows", [1, 8 * 512], F32)
        s_ceps = sb("s_ceps", [128, 1], F32)

        def row(i):
            return s_rows[0:1, i * 512:(i + 1) * 512]

        nc.vector.memset(s_ones[:], 1.0)
        nc.vector.memset(s_ones1[:], 1.0)
        nc.vector.memset(s_ceps[:], EPS)
        nc.const_aps.register(F32, EPS, s_ceps[:])

        for dst, src in ((s_scg, d_scg), (s_scu, d_scu), (s_scd, d_scd)):
            ins = nc.sync.dma_start(dst[:], src.ap())
            p.inc(ins, "one", 16)
        nc.sync.wait_ge(p.sems["one"], p.c("one"))
        nc.all_engine_barrier()

        with ExitStack() as att:
            sba = lambda n, s, d: att.enter_context(nc.sbuf_tensor(n, s, d))
            s_kt = sba("s_kt", [128, W], BF16)
            s_v = sba("s_v", [128, W], BF16)
            s_qt = sba("s_qt", [128, NHEADS * R], BF16)

            with ExitStack() as ph:
                sbp = lambda n, s, d: ph.enter_context(nc.sbuf_tensor(n, s, d))
                s_xlo = sbp("s_xlo", [128, HC * R], BF16)
                s_wq = sbp("s_wq", [128, 2 * HC * 128], BF16)
                s_wkv = sbp("s_wkv", [128, HC * 256], BF16)
                s_xf = sbp("s_xf", [128, 2 * 512], F32)
                s_xb = sbp("s_xb", [128, 2 * 512], BF16)
                s_xq = sbp("s_xq", [128, 2 * 512], BF16)
                s_xc = sbp("s_xc", [128, 4 * 512], BF16)
                s_tn = sbp("s_tn", [128, 512], F32)

                ins = nc.gpsimd.dma_start(s_wkv[:], d_wkv.ap())
                wkv_rdy = p.inc(ins, "oneg", 16)
                psp = lambda n, s: ph.enter_context(nc.psum_tensor(n, s, F32))
                P_s1 = psp("p_s1", [128, 512])
                P_s2 = psp("p_s2", [128, 512])
                P_br = psp("p_br", [128, 512])
                P_bc = psp("p_bc", [128, 512])
                p_kt = psp("p_kt", [128, 512])
                p_v = psp("p_v", [128, 512])
                P_q = psp("p_q", [128, 1024])

                xsrc = d_xkv.ap().rearrange("(c p) n -> p c n", p=128)
                PTSEQ = [6, 7, 0, 1, 2, 3, 4, 5]

                def emit_passA(pt, eidx):
                    cols = slice(pt * 512, pt * 512 + 512)
                    for g4 in range(HC):
                        gpar = g4 % 2
                        xb = s_xb[:, gpar * 512:(gpar + 1) * 512]
                        xq = s_xq[:, gpar * 512:(gpar + 1) * 512]
                        gidx = eidx * HC + g4
                        if gidx >= 2:
                            p.wait(nc.gpsimd, "mm1g", gidx - 1)
                        xbs = "xb0" if gidx % 2 == 0 else "xb1"
                        ins = nc.gpsimd.dma_start(xb, xsrc[:, g4, cols])
                        rdy = p.inc(ins, xbs, 16)
                        nc.scalar.wait_ge(p.sems[xbs], rdy)
                        if gidx >= 2:
                            p.wait(nc.scalar, "mm1g", gidx - 1)
                        ins = nc.scalar.activation(xq, xb, AF.Square)
                        p.inc(ins, "sqg")
                        p.wait(nc.tensor, "sqg", gidx + 1)
                        if g4 == 0 and eidx >= 1:
                            p.wait(nc.tensor, "st1", eidx)
                        hc = g4
                        nc.tensor.matmul(
                            P_s1[0:1, :], s_ones[:], xb,
                            start=(hc == 0), stop=(hc == HC - 1))
                        ins = nc.tensor.matmul(
                            P_s2[0:1, :], s_ones[:], xq,
                            start=(hc == 0), stop=(hc == HC - 1))
                        p.inc(ins, "mm1g")

                def emit_rows(pt, eidx):
                    par = eidx % 2
                    r_m, r_rs, r_c = row(par * 3), row(1 + par * 3), row(2 + par * 3)
                    r_e2, r_v, r_ln = row(6), row(7), row(6)
                    p.wait(nc.vector, "mm1g", HC * (eidx + 1))
                    nc.vector.tensor_scalar_mul(r_m, P_s1[0:1, :], 1.0 / H)
                    nc.vector.tensor_scalar_mul(r_e2, P_s2[0:1, :], 1.0 / H)
                    nc.vector.tensor_mul(r_v, r_m, r_m)
                    ins = nc.vector.tensor_sub(r_v, r_e2, r_v)
                    p.inc(ins, "st1")
                    p.wait(nc.scalar, "st1", eidx + 1)
                    nc.scalar.activation(r_ln, r_v, AF.Ln, bias=EPS)
                    ins = nc.scalar.activation(r_rs, r_ln, AF.Exp, scale=-0.5)
                    p.inc(ins, "rsd")
                    p.wait(nc.vector, "rsd", eidx + 1)
                    ins = nc.vector.scalar_tensor_tensor(
                        r_c, r_m, -1.0, r_rs, op0=ALU.mult, op1=ALU.mult)
                    p.inc(ins, "st2")
                    p.wait(nc.tensor, "st2", eidx + 1)
                    if eidx >= 1:
                        p.wait(nc.tensor, "dv", eidx * HC)
                    nc.tensor.matmul(P_br[:], s_ones1[:], r_rs,
                                     start=True, stop=True)
                    ins = nc.tensor.matmul(P_bc[:], s_ones1[:], r_c,
                                           start=True, stop=True)
                    p.inc(ins, "bcd")

                def q_pieces():
                    wq_rdy = [0] * NHEADS
                    ins = nc.gpsimd.dma_start(s_wq[:, 0:HC * 128], d_wq.ap()[0])
                    wq_rdy[0] = p.inc(ins, "wt0", 16)
                    for ct in range(NHEADS):
                        par = ct % 2
                        wq = s_wq[:, par * HC * 128:(par + 1) * HC * 128]
                        if ct + 1 < NHEADS:
                            if ct >= 1:
                                p.wait(nc.gpsimd, "qmm", ct)
                            ins = nc.gpsimd.dma_start(
                                s_wq[:, (1 - par) * HC * 128:(2 - par) * HC * 128],
                                d_wq.ap()[ct + 1])
                            wq_rdy[ct + 1] = p.inc(
                                ins, "wt0" if (ct + 1) % 2 == 0 else "wt1", 16)
                        nc.tensor.wait_ge(
                            p.sems["wt0" if ct % 2 == 0 else "wt1"], wq_rdy[ct])
                        if ct == 0:
                            p.wait(nc.tensor, "dv", 2 * HC)
                        else:
                            p.wait(nc.tensor, "qev", ct)
                        for piece in range(8):
                            for hc in range(piece * 4, piece * 4 + 4):
                                for hf in range(2):
                                    ins = nc.tensor.matmul(
                                        P_q[:, hf * 512:(hf + 1) * 512],
                                        wq[:, hc * 128:(hc + 1) * 128],
                                        s_xlo[:, hc * R + hf * 512:hc * R + hf * 512 + 512],
                                        start=(hc == 0), stop=(hc == HC - 1))
                            if piece < 7:
                                yield
                        p.inc(ins, "qmm")
                        p.wait(nc.scalar, "qmm", ct + 1)
                        nc.scalar.activation(
                            s_qt[:, ct * R:ct * R + 512], P_q[:, 0:512],
                            AF.Copy, scale=QK_SCALE)
                        ins = nc.scalar.activation(
                            s_qt[:, ct * R + 512:ct * R + 1024], P_q[:, 512:1024],
                            AF.Copy, scale=QK_SCALE)
                        p.inc(ins, "qev")
                        yield

                qgen = q_pieces()
                q_done = [False]

                def pump(n):
                    for _ in range(n):
                        if q_done[0]:
                            return
                        try:
                            next(qgen)
                        except StopIteration:
                            q_done[0] = True
                            return

                def emit_passCD(pt, eidx):
                    own = pt >= 6
                    cols = slice(pt * 512, pt * 512 + 512)
                    nchunk = eidx * HC
                    if eidx == 0:
                        nc.tensor.wait_ge(p.sems["oneg"], wkv_rdy)
                    for g4 in range(HC):
                        gidx = eidx * HC + g4
                        xf = s_xf[:, (gidx % 2) * 512:(gidx % 2 + 1) * 512]
                        if gidx >= 2:
                            p.wait(nc.sync, "dv", gidx - 1)
                        xfs = f"xf{gidx % 2}"
                        ins = nc.sync.dma_start(xf, xsrc[:, g4, cols])
                        rdy = p.inc(ins, xfs, 16)
                        hc = g4
                        ch = nchunk + hc
                        if own:
                            q0 = (pt - 6) * 512
                            xd = s_xlo[:, hc * R + q0:hc * R + q0 + 512]
                        else:
                            xd = s_xc[:, (ch % 4) * 512:(ch % 4 + 1) * 512]
                        nc.vector.wait_ge(p.sems[xfs], rdy)
                        if g4 == 0:
                            p.wait(nc.vector, "bcd", eidx + 1)
                        if not own and ch >= 4:
                            p.wait(nc.vector, "kvc", ch - 3)
                        nc.vector.scalar_tensor_tensor(
                            s_tn[:], xf, 1.0,
                            P_br[:], op0=ALU.mult, op1=ALU.mult)
                        ins = nc.vector.scalar_tensor_tensor(
                            xd, s_tn[:], 1.0, P_bc[:],
                            op0=ALU.mult, op1=ALU.add)
                        p.inc(ins, "dv")
                        p.wait(nc.tensor, "dv", p.c("dv"))
                        if hc == 0 and eidx >= 1:
                            p.wait(nc.tensor, "kvx", eidx)
                        nc.tensor.matmul(
                            p_kt[:], s_wkv[:, hc * 256:hc * 256 + 128],
                            xd, start=(hc == 0), stop=(hc == HC - 1))
                        for sub in range(4):
                            ins = nc.tensor.matmul(
                                p_v[:, sub * 128:(sub + 1) * 128],
                                xd[:, sub * 128:(sub + 1) * 128],
                                s_wkv[:, hc * 256 + 128:hc * 256 + 256],
                                start=(hc == 0 and sub == 0),
                                stop=(hc == HC - 1))
                        p.inc(ins, "kvc")
                        if eidx >= 2:
                            pump(1)
                    p.wait(nc.vector, "kvc", nchunk + HC)
                    nc.vector.tensor_copy(s_kt[:, pt * 512:(pt + 1) * 512], p_kt[:])
                    ins = nc.vector.tensor_copy(s_v[:, pt * 512:(pt + 1) * 512], p_v[:])
                    p.inc(ins, "kvx")

                emit_passA(PTSEQ[0], 0)
                for eidx, pt in enumerate(PTSEQ):
                    emit_rows(pt, eidx)
                    if eidx + 1 < 8:
                        emit_passA(PTSEQ[eidx + 1], eidx + 1)
                    emit_passCD(pt, eidx)
                pump(10000)

                if "kt" in d_taps:
                    p.wait(nc.sync, "kvx", 8)
                    ins = nc.sync.dma_start(d_taps["kt"].ap(), s_kt[:])
                    p.inc(ins, "one", 16)
                if "v" in d_taps:
                    p.wait(nc.sync, "kvx", 8)
                    ins = nc.sync.dma_start(d_taps["v"].ap(), s_v[:])
                    p.inc(ins, "one", 16)
                if "qt" in d_taps:
                    p.wait(nc.sync, "qev", NHEADS)
                    ins = nc.sync.dma_start(d_taps["qt"].ap(), s_qt[:])
                    p.inc(ins, "one", 16)
                p.wait(nc.vector, "kvx", 8)
                drain_dma()
                nc.all_engine_barrier()

            with ExitStack() as p34:
                s_at = p34.enter_context(nc.sbuf_tensor("s_at", [128, NHEADS * R], BF16))

                with ExitStack() as ph3:
                    sb3 = lambda n, s, d: ph3.enter_context(nc.sbuf_tensor(n, s, d))
                    s_bias = sb3("s_bias", [128, 2 * NHEADS * 32], F32)
                    s_dt = sb3("s_dt", [128, 2 * 5 * 512], F32)
                    s_ei = sb3("s_ei", [128, 4 * 512], F32)
                    s_pb = sb3("s_pb", [128, 4 * 512], BF16)
                    s_rc = sb3("s_rc", [128, 2 * 512], F32)
                    psp = lambda n, s: ph3.enter_context(nc.psum_tensor(n, s, F32))
                    PS = [psp(f"p_sc{i}", [128, 512]) for i in range(4)]
                    PC = [psp("p_c0", [128, 512]), psp("p_c1", [128, 512])]
                    PD = [psp("p_d0", [128, 512]), psp("p_d1", [128, 512])]

                    ins = nc.sync.dma_start(s_bias[:], d_bias.ap())
                    bias_rdy = p.inc(ins, "one", 16)
                    dt_rdy = [0] * NHEADS
                    ins = nc.gpsimd.dma_start(
                        s_dt[:, 0:5 * 512].rearrange("p (f n) -> p f n", f=5),
                        d_dt.ap()[0].rearrange("f p n -> p f n"))
                    dt_rdy[0] = p.inc(ins, "dd0", 16)

                    slopes_f = _alibi_slopes(NHEADS)
                    adv0 = p.c("adv")

                    slots = []
                    pairs = []
                    for h in range(NHEADS):
                        for qt in range(2):
                            ns = NSLOT[qt]
                            j0 = _slot_start(slopes_f, h, qt)
                            first = len(slots)
                            for j in range(j0, ns):
                                diag = j - (ns - 4)
                                slots.append(dict(
                                    h=h, qt=qt, j=j, pair=len(pairs),
                                    dt_off=(h % 2) * 5 * 512 +
                                           (0 if diag < 0 else (1 + diag) * 512),
                                    bias_col=qt * (NHEADS * 32) + h * 32 + j,
                                    first_of_head=(qt == 0 and j == j0),
                                ))
                            pairs.append(dict(h=h, qt=qt, first=first,
                                              last=len(slots) - 1,
                                              cpar=len(pairs) % 2))
                    S_total = len(slots)
                    last_to_pair = {pr["last"]: pr for pr in pairs}
                    adv_b4_head = [adv0] * (NHEADS + 1)
                    for pr in pairs:
                        if pr["qt"] == 1:
                            adv_b4_head[pr["h"] + 1] = adv0 + pr["last"] + 1

                    dt_prefetch_due = {}
                    for h in range(1, NHEADS):
                        due = adv_b4_head[h - 1] - adv0 if h >= 2 else 0
                        dt_prefetch_due.setdefault(max(due, 0), []).append(h)

                    def issue_dt(h):
                        dpar2 = h % 2
                        if h >= 2:
                            p.wait(nc.gpsimd, "adv", adv_b4_head[h - 1])
                        ins = nc.gpsimd.dma_start(
                            s_dt[:, dpar2 * 5 * 512:(dpar2 + 1) * 5 * 512]
                            .rearrange("p (f n) -> p f n", f=5),
                            d_dt.ap()[h].rearrange("f p n -> p f n"))
                        dt_rdy[h] = p.inc(ins, "dd0" if h % 2 == 0 else "dd1", 16)

                    pe_bcast_due = {}
                    for pr in pairs:
                        pe_bcast_due.setdefault(min(pr["last"] + 2, S_total - 1),
                                                []).append(pr)
                    pv_at = [0] * len(pairs)
                    for g in range(S_total + 3):
                        if g < S_total:
                            sl = slots[g]
                            if g >= 4:
                                p.wait(nc.tensor, "adv", adv0 + g - 3)
                            ins = nc.tensor.matmul(
                                PS[g % 4][:],
                                s_kt[:, sl["j"] * 128:(sl["j"] + 1) * 128],
                                s_qt[:, sl["h"] * R + sl["qt"] * 512:
                                     sl["h"] * R + sl["qt"] * 512 + 512],
                                start=True, stop=True)
                            p.inc(ins, "sc")
                        k = g - 3
                        if k >= 0:
                            sl = slots[k]
                            pr = pairs[sl["pair"]]
                            cpar = pr["cpar"]
                            p.wait(nc.tensor, "ex", k + 1)
                            if k == pr["first"] and sl["pair"] >= 2:
                                p.wait(nc.tensor, "ett", sl["pair"] - 1)
                                p.wait(nc.tensor, "ecp", sl["pair"] - 1)
                            nc.tensor.matmul(
                                PC[cpar][:],
                                s_v[:, sl["j"] * 128:(sl["j"] + 1) * 128],
                                s_pb[:, (k % 4) * 512:(k % 4 + 1) * 512],
                                start=(k == pr["first"]), stop=(k == pr["last"]))
                            ins = nc.tensor.matmul(
                                PD[cpar][0:1, :], s_ones[:],
                                s_pb[:, (k % 4) * 512:(k % 4 + 1) * 512],
                                start=(k == pr["first"]), stop=(k == pr["last"]))
                            pv_at[sl["pair"]] = p.inc(ins, "pv")
                            for pr2 in pe_bcast_due.get(k, ()):
                                p.wait(nc.tensor, "edv", pairs.index(pr2) + 1)
                                ins = nc.tensor.matmul(
                                    PD[pr2["cpar"]][:], s_ones1[:],
                                    row(pr2["cpar"]), start=True, stop=True)
                                p.inc(ins, "epe")

                    dve_copy_due = {}
                    dve_fin_due = {}
                    for pi, pr in enumerate(pairs):
                        dve_copy_due.setdefault(min(pr["last"] + 1, S_total - 1),
                                                []).append(pi)
                        dve_fin_due.setdefault(min(pr["last"] + 4, S_total - 1),
                                               []).append(pi)

                    def dve_copy(pi):
                        pr = pairs[pi]
                        p.wait(nc.vector, "pv", pv_at[pi])
                        if pi >= 2:
                            p.wait(nc.vector, "epe", pi - 1)
                        ins = nc.vector.tensor_copy(row(pr["cpar"]),
                                                    PD[pr["cpar"]][0:1, :])
                        p.inc(ins, "edv")

                    def dve_fin(pi):
                        pr = pairs[pi]
                        p.wait(nc.vector, "epe", pi + 1)
                        ins = nc.vector.reciprocal(
                            s_rc[:, pr["cpar"] * 512:(pr["cpar"] + 1) * 512],
                            PD[pr["cpar"]][:])
                        p.inc(ins, "ecp")
                        ins = nc.vector.scalar_tensor_tensor(
                            s_at[:, pr["h"] * R + pr["qt"] * 512:
                                 pr["h"] * R + pr["qt"] * 512 + 512],
                            PC[pr["cpar"]][:], 1.0,
                            s_rc[:, pr["cpar"] * 512:(pr["cpar"] + 1) * 512],
                            op0=ALU.mult, op1=ALU.mult)
                        p.inc(ins, "ett")

                    for g in range(S_total):
                        sl = slots[g]
                        for h2 in dt_prefetch_due.get(g, ()):
                            issue_dt(h2)
                        if sl["first_of_head"]:
                            nc.vector.wait_ge(
                                p.sems["dd0" if sl["h"] % 2 == 0 else "dd1"],
                                dt_rdy[sl["h"]])
                            if sl["h"] == 0:
                                nc.vector.wait_ge(p.sems["one"], bias_rdy)
                        p.wait(nc.vector, "sc", g + 1)
                        if g >= 4:
                            p.wait(nc.vector, "ex", g - 3)
                        ins = nc.vector.scalar_tensor_tensor(
                            s_ei[:, (g % 4) * 512:(g % 4 + 1) * 512],
                            PS[g % 4][:],
                            s_bias[:, sl["bias_col"]:sl["bias_col"] + 1],
                            s_dt[:, sl["dt_off"]:sl["dt_off"] + 512],
                            op0=ALU.add, op1=ALU.add)
                        p.inc(ins, "adv")
                        for pi in dve_copy_due.get(g, ()):
                            dve_copy(pi)
                        for pi in dve_fin_due.get(g, ()):
                            dve_fin(pi)

                    for g in range(S_total):
                        p.wait(nc.scalar, "adv", adv0 + g + 1)
                        if g >= 4:
                            p.wait(nc.scalar, "pv", g - 3)
                        ins = nc.scalar.activation(
                            s_pb[:, (g % 4) * 512:(g % 4 + 1) * 512],
                            s_ei[:, (g % 4) * 512:(g % 4 + 1) * 512], AF.Exp)
                        p.inc(ins, "ex")

                    if "at" in d_taps:
                        p.wait(nc.sync, "ett", len(pairs))
                        ins = nc.sync.dma_start(d_taps["at"].ap(), s_at[:])
                        p.inc(ins, "one", 16)
                    drain_dma()
                    nc.all_engine_barrier()

                with ExitStack() as ph4:
                    sb4 = lambda n, s, d: ph4.enter_context(nc.sbuf_tensor(n, s, d))
                    ps4 = lambda n, s: ph4.enter_context(nc.psum_tensor(n, s, F32))
                    s_wo = sb4("s_wo", [128, 2 * HC * 128], BF16)
                    s_xr = sb4("s_xr", [128, 2 * R], F32)
                    s_x1 = sb4("s_x1", [128, 2 * R], F32)
                    s_xb16 = sb4("s_xb16", [128, 2 * R], BF16)
                    s_xsq = sb4("s_xsq", [128, 2 * R], BF16)
                    P_o = [ps4(f"p_o{i}", [128, 512]) for i in range(4)]
                    P_s1 = ps4("p4_s1", [128, 1024])
                    P_s2 = ps4("p4_s2", [128, 1024])
                    wo_rdy = [0] * OT_D
                    xr_rdy = [0] * OT_D
                    x1w_at = [0] * OT_D
                    ins = nc.gpsimd.dma_start(s_wo[:, 0:HC * 128], d_wo.ap()[0])
                    wo_rdy[0] = p.inc(ins, "wt0", 16)
                    for ot in range(OT_D):
                        par = ot % 2
                        wo = s_wo[:, par * HC * 128:(par + 1) * HC * 128]
                        if ot >= 2:
                            p.wait(nc.sync, "oev", ot - 1)
                        ins = nc.sync.dma_start(
                            s_xr[:, par * R:(par + 1) * R],
                            d_xkv.ap()[ot * 128:(ot + 1) * 128, QBASE:QBASE + R])
                        xr_rdy[ot] = p.inc(ins, "xf0" if par == 0 else "xf1", 16)
                        if ot + 1 < OT_D:
                            if ot >= 1:
                                p.wait(nc.gpsimd, "omm", ot)
                            ins = nc.gpsimd.dma_start(
                                s_wo[:, (1 - par) * HC * 128:(2 - par) * HC * 128],
                                d_wo.ap()[ot + 1])
                            wo_rdy[ot + 1] = p.inc(ins, "wt0" if (ot + 1) % 2 == 0 else "wt1", 16)
                        nc.tensor.wait_ge(p.sems["wt0" if par == 0 else "wt1"], wo_rdy[ot])
                        if ot >= 2:
                            p.wait(nc.tensor, "oev", ot - 1)
                        for hc in range(HC):
                            for hf in range(2):
                                ins = nc.tensor.matmul(
                                    P_o[2 * par + hf][:],
                                    wo[:, hc * 128:(hc + 1) * 128],
                                    s_at[:, hc * R + hf * 512:hc * R + hf * 512 + 512],
                                    start=(hc == 0), stop=(hc == HC - 1))
                        p.inc(ins, "omm")
                        p.wait(nc.vector, "omm", ot + 1)
                        nc.vector.wait_ge(p.sems["xf0" if par == 0 else "xf1"], xr_rdy[ot])
                        if ot >= 2:
                            nc.vector.wait_ge(p.sems["st0" if par == 0 else "st1x"],
                                              x1w_at[ot - 2])
                            p.wait(nc.vector, "q4s", ot - 1)
                        for hf in range(2):
                            ins = nc.vector.scalar_tensor_tensor(
                                s_x1[:, par * R + hf * 512:par * R + hf * 512 + 512],
                                P_o[2 * par + hf][:], 1.0,
                                s_xr[:, par * R + hf * 512:par * R + hf * 512 + 512],
                                op0=ALU.mult, op1=ALU.add)
                        p.inc(ins, "oev")
                        if ot >= 2:
                            p.wait(nc.vector, "m5", ot - 1)
                        ins = nc.vector.tensor_copy(
                            s_xb16[:, par * R:(par + 1) * R],
                            s_x1[:, par * R:(par + 1) * R])
                        p.inc(ins, "q5")
                        p.wait(nc.scalar, "oev", ot + 1)
                        if ot >= 2:
                            p.wait(nc.scalar, "m5", ot - 1)
                        ins = nc.scalar.activation(
                            s_xsq[:, par * R:(par + 1) * R],
                            s_x1[:, par * R:(par + 1) * R], AF.Square)
                        p.inc(ins, "q4s")
                        p.wait(nc.tensor, "q5", ot + 1)
                        p.wait(nc.tensor, "q4s", ot + 1)
                        for hf in range(2):
                            nc.tensor.matmul(
                                P_s1[0:1, hf * 512:(hf + 1) * 512], s_ones[:],
                                s_xb16[:, par * R + hf * 512:par * R + hf * 512 + 512],
                                start=(ot == 0), stop=(ot == OT_D - 1))
                            ins = nc.tensor.matmul(
                                P_s2[0:1, hf * 512:(hf + 1) * 512], s_ones[:],
                                s_xsq[:, par * R + hf * 512:par * R + hf * 512 + 512],
                                start=(ot == 0), stop=(ot == OT_D - 1))
                        p.inc(ins, "m5")
                        p.wait(nc.sync, "oev", ot + 1)
                        ins = nc.sync.dma_start(d_x1.ap()[ot],
                                                s_x1[:, par * R:(par + 1) * R])
                        x1w_at[ot] = p.inc(ins, "st0" if par == 0 else "st1x", 16)
                    p.wait(nc.vector, "m5", OT_D)
                    for hf in range(2):
                        r_rs, r_c = row(hf * 2), row(1 + hf * 2)
                        r_m, r_e2, r_v, r_ln = row(4), row(6), row(7), row(6)
                        nc.vector.tensor_scalar_mul(
                            r_m, P_s1[0:1, hf * 512:(hf + 1) * 512], 1.0 / H)
                        nc.vector.tensor_scalar_mul(
                            r_e2, P_s2[0:1, hf * 512:(hf + 1) * 512], 1.0 / H)
                        nc.vector.tensor_mul(r_v, r_m, r_m)
                        ins = nc.vector.tensor_sub(r_v, r_e2, r_v)
                        p.inc(ins, "r5a")
                        p.wait(nc.scalar, "r5a", hf + 1)
                        nc.scalar.activation(r_ln, r_v, AF.Ln, bias=EPS)
                        ins = nc.scalar.activation(r_rs, r_ln, AF.Exp, scale=-0.5)
                        p.inc(ins, "r5b")
                        p.wait(nc.vector, "r5b", hf + 1)
                        ins = nc.vector.scalar_tensor_tensor(
                            r_c, r_m, -1.0, r_rs, op0=ALU.mult, op1=ALU.mult)
                        p.inc(ins, "r5c")
                    drain_dma()
                    nc.all_engine_barrier()

        with ExitStack() as p5:
            s_xl1 = p5.enter_context(nc.sbuf_tensor("s_xl1", [128, HC * R], BF16))

            with ExitStack() as phb:
                sbb = lambda n, s, d: phb.enter_context(nc.sbuf_tensor(n, s, d))
                psb = lambda n, s: phb.enter_context(nc.psum_tensor(n, s, F32))
                s_gw = sbb("s_gw", [128, 2 * HC * 128], BF16)
                s_uw = sbb("s_uw", [128, 2 * HC * 128], BF16)
                s_sg = sbb("s_sg", [128, 2 * R], F32)
                s_su = sbb("s_su", [128, 2 * R], F32)
                s_ht = sbb("s_ht", [128, 2 * R], BF16)
                P_g = [psb(f"p_g{i}", [128, 512]) for i in range(4)]
                P_u = [psb(f"p_u{i}", [128, 512]) for i in range(4)]
                s_x1f = sbb("s_x1f", [128, 4 * R], F32)
                s_tn = sbb("s_tn2", [128, 512], F32)
                s_bb = sbb("s_bb", [128, 4 * 512], F32)
                for hf in range(2):
                    nc.tensor.matmul(P_g[hf][:], s_ones1[:], row(hf * 2),
                                     start=True, stop=True)
                    ins = nc.tensor.matmul(P_g[2 + hf][:], s_ones1[:],
                                           row(1 + hf * 2), start=True, stop=True)
                    p.inc(ins, "b5")
                p.wait(nc.vector, "b5", 2)
                for i in range(4):
                    ins = nc.vector.tensor_copy(s_bb[:, i * 512:(i + 1) * 512],
                                                P_g[i][:])
                p.inc(ins, "r5a")
                b5e_at = p.c("r5a")

                for hc in range(HC):
                    par4 = hc % 4
                    if hc >= 4:
                        p.wait(nc.sync, "n5", hc - 3)
                    xfs = f"xf{hc % 4}"
                    ins = nc.sync.dma_start(s_x1f[:, par4 * R:(par4 + 1) * R],
                                            d_x1.ap()[hc])
                    rdy = p.inc(ins, xfs, 16)
                    nc.vector.wait_ge(p.sems[xfs], rdy)
                    for hf in range(2):
                        nc.vector.scalar_tensor_tensor(
                            s_tn[:],
                            s_x1f[:, par4 * R + hf * 512:par4 * R + hf * 512 + 512],
                            1.0, s_bb[:, hf * 512:(hf + 1) * 512],
                            op0=ALU.mult, op1=ALU.mult)
                        ins = nc.vector.scalar_tensor_tensor(
                            s_xl1[:, hc * R + hf * 512:hc * R + hf * 512 + 512],
                            s_tn[:], 1.0, s_bb[:, (2 + hf) * 512:(3 + hf) * 512],
                            op0=ALU.mult, op1=ALU.add)
                    p.inc(ins, "n5")
                if "xl1" in d_taps:
                    p.wait(nc.sync, "n5", HC)
                    ins = nc.sync.dma_start(d_taps["xl1"].ap(), s_xl1[:])
                    p.inc(ins, "one", 16)
                if "rows" in d_taps:
                    ins = nc.sync.dma_start(d_taps["rows"].ap(), s_rows[:])
                    p.inc(ins, "one", 16)
                gw_rdy = [0] * OT_G
                htw_at = [0] * OT_G
                nc.gpsimd.dma_start(s_gw[:, 0:HC * 128], d_gw.ap()[0]) \
                    .then_inc(p.sems["wt0"], 16)
                p.cnt["wt0"] += 16
                ins = nc.gpsimd.dma_start(s_uw[:, 0:HC * 128], d_uw.ap()[0])
                gw_rdy[0] = p.inc(ins, "wt0", 16)
                for ot in range(OT_G):
                    par = ot % 2
                    gw = s_gw[:, par * HC * 128:(par + 1) * HC * 128]
                    uw = s_uw[:, par * HC * 128:(par + 1) * HC * 128]
                    if ot + 1 < OT_G:
                        if ot >= 1:
                            p.wait(nc.gpsimd, "mmg", ot)
                        nwt = "wt0" if (ot + 1) % 2 == 0 else "wt1"
                        nc.gpsimd.dma_start(
                            s_gw[:, (1 - par) * HC * 128:(2 - par) * HC * 128],
                            d_gw.ap()[ot + 1]).then_inc(p.sems[nwt], 16)
                        p.cnt[nwt] += 16
                        ins = nc.gpsimd.dma_start(
                            s_uw[:, (1 - par) * HC * 128:(2 - par) * HC * 128],
                            d_uw.ap()[ot + 1])
                        gw_rdy[ot + 1] = p.inc(ins, nwt, 16)
                    nc.tensor.wait_ge(p.sems["wt0" if par == 0 else "wt1"], gw_rdy[ot])
                    if ot >= 2:
                        p.wait(nc.tensor, "evg", ot - 1)
                    if ot == 0:
                        p.wait(nc.tensor, "r5a", b5e_at)
                    for hc in range(HC):
                        if ot == 0:
                            p.wait(nc.tensor, "n5", hc + 1)
                        elif ot == 1 and hc == 0:
                            p.wait(nc.tensor, "n5", HC)
                        for hf in range(2):
                            nc.tensor.matmul(
                                P_g[2 * par + hf][:],
                                gw[:, hc * 128:(hc + 1) * 128],
                                s_xl1[:, hc * R + hf * 512:hc * R + hf * 512 + 512],
                                start=(hc == 0), stop=(hc == HC - 1))
                    if ot == 0:
                        p.wait(nc.tensor, "n5", HC)
                    for hc in range(HC):
                        for hf in range(2):
                            ins = nc.tensor.matmul(
                                P_u[2 * par + hf][:],
                                uw[:, hc * 128:(hc + 1) * 128],
                                s_xl1[:, hc * R + hf * 512:hc * R + hf * 512 + 512],
                                start=(hc == 0), stop=(hc == HC - 1))
                    p.inc(ins, "mmg")
                    p.wait(nc.scalar, "mmg", ot + 1)
                    if ot >= 2:
                        p.wait(nc.scalar, "hdv", ot - 1)
                    for hf in range(2):
                        nc.scalar.activation(
                            s_sg[:, par * R + hf * 512:par * R + hf * 512 + 512],
                            P_g[2 * par + hf][:], AF.Silu,
                            scale=s_scg[:, ot:ot + 1])
                        ins = nc.scalar.activation(
                            s_su[:, par * R + hf * 512:par * R + hf * 512 + 512],
                            P_u[2 * par + hf][:], AF.Copy,
                            scale=s_scu[:, ot:ot + 1])
                    p.inc(ins, "evg")
                    p.wait(nc.vector, "evg", ot + 1)
                    if ot >= 2:
                        nc.vector.wait_ge(p.sems["st0" if par == 0 else "st1x"],
                                          htw_at[ot - 2])
                    ins = nc.vector.tensor_mul(s_ht[:, par * R:(par + 1) * R],
                                               s_sg[:, par * R:(par + 1) * R],
                                               s_su[:, par * R:(par + 1) * R])
                    p.inc(ins, "hdv")
                    p.wait(nc.sync, "hdv", ot + 1)
                    sts = "st0" if par == 0 else "st1x"
                    for hf in range(2):
                        ins = nc.sync.dma_start(
                            d_hs[hf].ap()[:, ot * 512:(ot + 1) * 512],
                            s_ht[:, par * R + hf * 512:par * R + hf * 512 + 512])
                        htw_at[ot] = p.inc(ins, sts, 16)
                drain_dma()
                nc.all_engine_barrier()

        with ExitStack() as ph:
            sbp = lambda n, s, d: ph.enter_context(nc.sbuf_tensor(n, s, d))
            psp = lambda n, s: ph.enter_context(nc.psum_tensor(n, s, F32))
            s_hf = sbp("s_hf", [128, OT_G * 512], BF16)
            s_dwb = sbp("s_dwb", [128, 2 * OT_G * 128], BF16)
            s_x1r = sbp("s_x1r", [128, 2 * 512], F32)
            s_ev = sbp("s_ev", [128, 2 * 512], F32)
            s_yo = sbp("s_yo", [128, 2 * 512], F32)
            P_y = [psp("p_y0", [128, 512]), psp("p_y1", [128, 512])]
            outw_at = [0] * (2 * OT_D)
            for half in range(2):
                p.wait(nc.sync, "mmd", p.c("mmd"))
                ins = nc.sync.dma_start(s_hf[:], d_hs[half].ap())
                hf_rdy = p.inc(ins, "one", 16)
                if half == 0:
                    p.wait(nc.gpsimd, "mmd", p.c("mmd"))
                    ins = nc.gpsimd.dma_start(s_dwb[:, 0:OT_G * 128], d_dw.ap()[0])
                    dw_rdy = p.inc(ins, "wt0", 16)
                    dw_sem = "wt0"
                ots = list(range(OT_D)) if half == 0 else list(range(OT_D - 1, -1, -1))
                for oi, ot in enumerate(ots):
                    m = half * OT_D + oi
                    par = ot % 2
                    dw = s_dwb[:, par * OT_G * 128:(par + 1) * OT_G * 128]
                    ndw_rdy = dw_rdy
                    ndw_sem = dw_sem
                    if oi + 1 < OT_D:
                        nxt = ots[oi + 1]
                        p.wait(nc.gpsimd, "mmd", m)
                        ndw_sem = "wt0" if nxt % 2 == 0 else "wt1"
                        ins = nc.gpsimd.dma_start(
                            s_dwb[:, (1 - par) * OT_G * 128:(2 - par) * OT_G * 128],
                            d_dw.ap()[nxt])
                        ndw_rdy = p.inc(ins, ndw_sem, 16)
                    if m >= 2:
                        p.wait(nc.sync, "ydv", m - 1)
                    xfs = "xf0" if par == 0 else "xf1"
                    ins = nc.sync.dma_start(
                        s_x1r[:, par * 512:(par + 1) * 512],
                        d_x1.ap()[ot][:, half * 512:half * 512 + 512])
                    x1r_rdy = p.inc(ins, xfs, 16)
                    nc.tensor.wait_ge(p.sems[dw_sem], dw_rdy)
                    nc.tensor.wait_ge(p.sems["one"], hf_rdy)
                    if m >= 2:
                        p.wait(nc.tensor, "evd", m - 1)
                    for ic in range(OT_G):
                        ins = nc.tensor.matmul(
                            P_y[par][:], dw[:, ic * 128:(ic + 1) * 128],
                            s_hf[:, ic * 512:(ic + 1) * 512],
                            start=(ic == 0), stop=(ic == OT_G - 1))
                    p.inc(ins, "mmd")
                    dw_rdy = ndw_rdy
                    dw_sem = ndw_sem
                    p.wait(nc.scalar, "mmd", m + 1)
                    if m >= 2:
                        p.wait(nc.scalar, "ydv", m - 1)
                    ins = nc.scalar.activation(
                        s_ev[:, par * 512:(par + 1) * 512], P_y[par][:],
                        AF.Copy, scale=s_scd[:, ot:ot + 1])
                    p.inc(ins, "evd")
                    p.wait(nc.vector, "evd", m + 1)
                    nc.vector.wait_ge(p.sems[xfs], x1r_rdy)
                    if m >= 2:
                        nc.vector.wait_ge(p.sems["st0" if par == 0 else "st1x"],
                                          outw_at[m - 2])
                    ins = nc.vector.scalar_tensor_tensor(
                        s_yo[:, par * 512:(par + 1) * 512],
                        s_ev[:, par * 512:(par + 1) * 512], 1.0,
                        s_x1r[:, par * 512:(par + 1) * 512],
                        op0=ALU.mult, op1=ALU.add)
                    p.inc(ins, "ydv")
                    p.wait(nc.sync, "ydv", m + 1)
                    ins = nc.sync.dma_start(
                        d_out.ap()[ot][:, half * 512:half * 512 + 512],
                        s_yo[:, par * 512:(par + 1) * 512])
                    outw_at[m] = p.inc(ins, "st0" if par == 0 else "st1x", 16)
            drain_dma()

    return nc



def _i8view(w_int32):
    w = np.ascontiguousarray(np.asarray(w_int32))
    return w.view(np.int8)[:, 0::4]


def _tile_lhsT(w):
    K, M = w.shape
    t = w.reshape(K // 128, 128, M // 128, 128)
    return np.ascontiguousarray(t.transpose(2, 1, 0, 3)).reshape(M // 128, 128, K)


def _prep_shared(wqkv, wo, gw8, uw8, dw8, gs, us, ds):
    slopes = _alibi_slopes(NHEADS)
    wq_t = _tile_lhsT(wqkv[:, :H])
    wkv_t = np.ascontiguousarray(
        wqkv[:, H:].reshape(HC, 128, 256).transpose(1, 0, 2)).reshape(128, HC * 256)
    wo_t = _tile_lhsT(wo)
    gw_t = _tile_lhsT(np.ascontiguousarray(gw8.T))
    uw_t = _tile_lhsT(np.ascontiguousarray(uw8.T))
    dw_t = _tile_lhsT(np.ascontiguousarray(dw8.T))
    scg_t = np.ascontiguousarray(gs.reshape(OT_G, 128).T)
    scu_t = np.ascontiguousarray(us.reshape(OT_G, 128).T)
    scd_t = np.ascontiguousarray(ds.reshape(OT_D, 128).T)

    dk = np.arange(128, dtype=np.float64)[:, None]
    dq = np.arange(512, dtype=np.float64)[None, :]
    dtiles = np.zeros((NHEADS, 5, 128, 512), np.float32)
    for h in range(NHEADS):
        sl = slopes[h]
        dtiles[h, 0] = (sl * (dk - dq)).astype(np.float32)
        for cls in range(4):
            a = sl * (dk - dq + 128.0 * cls)
            a = np.where((dk - dq + 128.0 * cls) <= 0, a, -60000.0)
            dtiles[h, 1 + cls] = a.astype(np.float32)
    return dict(wqt=wq_t, wkvt=wkv_t, wot=wo_t, gwt=gw_t, uwt=uw_t, dwt=dw_t,
                scg=scg_t, scu=scu_t, scd=scd_t, dtiles=dtiles), slopes


def _core_inputs(x, shared, slopes, c):
    b = c // 4
    r0 = R * (c % 4)
    shift = QBASE - r0
    xT = np.ascontiguousarray(x[b].T)
    xkv = np.zeros((H, W), np.float32)
    xkv[:, shift:] = xT[:, :W - shift]

    biasv = np.zeros((2, NHEADS, 32), np.float64)
    for qt in range(2):
        ns = NSLOT[qt]
        q0c = QBASE + qt * 512
        for h in range(NHEADS):
            for j in range(32):
                if j >= ns:
                    biasv[qt, h, j] = NEGBIG
                elif 128 * j + 128 <= shift:
                    biasv[qt, h, j] = NEGBIG
                elif j >= ns - 4:
                    biasv[qt, h, j] = SHIFT
                else:
                    biasv[qt, h, j] = SHIFT + slopes[h] * (128 * j - q0c)
    bias_in = np.ascontiguousarray(np.broadcast_to(
        biasv.reshape(-1).astype(np.float32)[None, :], (128, 2 * NHEADS * 32)))
    d = dict(shared)
    d["xkv"] = xkv
    d["biasv"] = bias_in
    return d


_CACHE = {}


def kernel(L0_attn_hidden_states, L0_attn_Wqkv, L0_attn_Wo,
           L0_ffn_gate_wq, L0_ffn_gate_scale, L0_ffn_up_wq, L0_ffn_up_scale,
           L0_ffn_down_wq, L0_ffn_down_scale):
    x = np.ascontiguousarray(np.asarray(L0_attn_hidden_states, dtype=np.float32))
    wqkv = np.ascontiguousarray(np.asarray(L0_attn_Wqkv, dtype=np.float32))
    wo = np.ascontiguousarray(np.asarray(L0_attn_Wo, dtype=np.float32))
    gw8 = _i8view(L0_ffn_gate_wq)
    uw8 = _i8view(L0_ffn_up_wq)
    dw8 = _i8view(L0_ffn_down_wq)
    gs = np.asarray(L0_ffn_gate_scale, dtype=np.float32)
    us = np.asarray(L0_ffn_up_scale, dtype=np.float32)
    ds = np.asarray(L0_ffn_down_scale, dtype=np.float32)

    shared, slopes = _prep_shared(wqkv, wo, gw8, uw8, dw8, gs, us, ds)
    in_maps = [_core_inputs(x, shared, slopes, c) for c in range(8)]

    if "nc" not in _CACHE:
        _CACHE["nc"] = build_program(TAPS)
    nc = _CACHE["nc"]

    res = run_bass_kernel_spmd(nc, in_maps, core_ids=list(range(8)))

    out = np.empty((B, S, H), np.float32)
    for c in range(8):
        b = c // 4
        r0 = R * (c % 4)
        yT = np.asarray(res.results[c]["out"], np.float32).reshape(H, R)
        out[b, r0:r0 + R, :] = yT.T
    if TAPS:
        kernel.last_results = res
    return out


if __name__ == "__main__":
    import time
    t0 = time.time()
    nc = build_program()
    print(f"build {time.time()-t0:.1f}s instructions: {nc.n_instructions()}")


# version 10 (final)
# speedup vs baseline: 1.0579x; 1.0579x over previous
import math
from contextlib import ExitStack

import numpy as np

import concourse.bass as bass
import concourse.mybir as mybir
from concourse.bass_utils import run_bass_kernel_spmd

F32 = mybir.dt.float32
BF16 = mybir.dt.bfloat16
I8 = mybir.dt.int8
AF = mybir.ActivationFunctionType
ALU = mybir.AluOpType

NHEADS = 32
H = 4096
HD = 128
INTER = 14336
B = 2
S = 4096
R = 1024
W = 4096
QBASE = W - R
NSLOT = (QBASE // 128 + 4, QBASE // 128 + 8)
EPS = 1e-5
QK_SCALE = 1.0 / math.sqrt(HD)
SHIFT = -45.0
NEGBIG = -1e30
HC = 32
OT_G = INTER // 128
OT_D = H // 128

TAPS = ()
ALIBI_CUT = 110.0


def _slot_start(slopes, h, qt):
    ns = NSLOT[qt]
    q0c = QBASE + qt * 512
    for j in range(ns):
        if j >= ns - 4 or slopes[h] * (q0c - 128 * j - 127) < ALIBI_CUT:
            return j
    return ns - 4


def _alibi_slopes(n):
    cp2 = 2 ** math.floor(math.log2(n))
    base = 2.0 ** (-(2.0 ** (-(math.log2(cp2) - 3))))
    slopes = base ** np.arange(1, cp2 + 1, dtype=np.float64)
    if cp2 != n:
        extra_base = 2.0 ** (-(2.0 ** (-(math.log2(2 * cp2) - 3))))
        rem = min(cp2, n - cp2)
        extra = extra_base ** np.arange(1, 1 + 2 * rem, 2, dtype=np.float64)
        slopes = np.concatenate([slopes, extra])
    return slopes




class Ctr:

    def __init__(self, nc, ctx, names):
        self.nc = nc
        self.sems = {n: ctx.enter_context(nc.semaphore(n)) for n in names}
        self.cnt = {n: 0 for n in names}

    def inc(self, ins, name, by=1):
        ins.then_inc(self.sems[name], by)
        self.cnt[name] += by
        return self.cnt[name]

    def c(self, name):
        return self.cnt[name]

    def wait(self, eng, name, val):
        if val > 0:
            eng.wait_ge(self.sems[name], val)


def build_program(taps=()):
    nc = bass.Bass("TRN2", target_bir_lowering=False)

    d_xkv = nc.dram_tensor("xkv", [H, W], F32, kind="ExternalInput")
    d_wq = nc.dram_tensor("wqt", [NHEADS, 128, HC * 128], F32, kind="ExternalInput")
    d_wkv = nc.dram_tensor("wkvt", [128, HC * 256], F32, kind="ExternalInput")
    d_wo = nc.dram_tensor("wot", [OT_D, 128, HC * 128], F32, kind="ExternalInput")
    d_gw = nc.dram_tensor("gwt", [OT_G, 128, HC * 128], I8, kind="ExternalInput")
    d_uw = nc.dram_tensor("uwt", [OT_G, 128, HC * 128], I8, kind="ExternalInput")
    d_dw = nc.dram_tensor("dwt", [OT_D, 128, OT_G * 128], I8, kind="ExternalInput")
    d_scg = nc.dram_tensor("scg", [128, OT_G], F32, kind="ExternalInput")
    d_scu = nc.dram_tensor("scu", [128, OT_G], F32, kind="ExternalInput")
    d_scd = nc.dram_tensor("scd", [128, OT_D], F32, kind="ExternalInput")
    d_dt = nc.dram_tensor("dtiles", [NHEADS, 5, 128, 512], F32, kind="ExternalInput")
    d_bias = nc.dram_tensor("biasv", [128, 2 * NHEADS * 32], F32, kind="ExternalInput")

    d_out = nc.dram_tensor("out", [OT_D, 128, R], F32, kind="ExternalOutput")
    x1_kind = "ExternalOutput" if "x1" in taps else "Internal"
    h_kind = "ExternalOutput" if "h" in taps else "Internal"
    d_x1 = nc.dram_tensor("x1s", [HC, 128, R], F32, kind=x1_kind)
    d_h0 = nc.dram_tensor("h0s", [128, OT_G * 512], BF16, kind=h_kind)
    d_h1 = nc.dram_tensor("h1s", [128, OT_G * 512], BF16, kind=h_kind)
    d_hs = (d_h0, d_h1)
    d_taps = {}
    for t, shape, dt_ in (("kt", [128, W], BF16), ("v", [128, W], BF16),
                          ("qt", [128, NHEADS * R], BF16),
                          ("at", [128, NHEADS * R], BF16),
                          ("xl1", [128, HC * R], BF16),
                          ("rows", [1, 8 * 512], F32)):
        if t in taps:
            d_taps[t] = nc.dram_tensor("tap_" + t, shape, dt_, kind="ExternalOutput")

    names = ["xf0", "xf1", "xf2", "xf3", "st0", "st1x", "one",
             "xb0", "xb1", "wt0", "wt1", "dd0", "dd1", "oneg",
             "sqg", "mm1g", "mm1", "st1", "rsd", "st2", "bcd",
             "dv", "kvc", "kvx", "qmm", "qev",
             "sc", "adv", "ex", "pv", "edv", "epe", "ecp", "ett",
             "omm", "oev",
             "q5", "q4s", "m5", "r5a", "r5b", "r5c", "b5", "n5",
             "mmg", "evg", "hdv", "mmd", "evd", "ydv", "fin"]

    with ExitStack() as top:
        p = Ctr(nc, top, names)
        sb = lambda n, s, d: top.enter_context(nc.sbuf_tensor(n, s, d))

        SY_SEMS = ("xf0", "xf1", "xf2", "xf3", "one")
        GP_SEMS = ("xb0", "xb1", "wt0", "wt1", "dd0", "dd1", "oneg",
                   "st0", "st1x")

        def drain_dma():
            for n_ in SY_SEMS:
                if p.c(n_):
                    nc.sync.wait_ge(p.sems[n_], p.c(n_))
            for n_ in GP_SEMS:
                if p.c(n_):
                    nc.gpsimd.wait_ge(p.sems[n_], p.c(n_))

        s_scg = sb("s_scg", [128, OT_G], F32)
        s_scu = sb("s_scu", [128, OT_G], F32)
        s_scd = sb("s_scd", [128, OT_D], F32)
        s_ones = sb("s_ones", [128, 1], BF16)
        s_ones1 = sb("s_ones1", [1, 128], F32)
        s_rows = sb("s_rows", [1, 8 * 512], F32)
        s_ceps = sb("s_ceps", [128, 1], F32)

        def row(i):
            return s_rows[0:1, i * 512:(i + 1) * 512]

        nc.vector.memset(s_ones[:], 1.0)
        nc.vector.memset(s_ones1[:], 1.0)
        nc.vector.memset(s_ceps[:], EPS)
        nc.const_aps.register(F32, EPS, s_ceps[:])

        for dst, src in ((s_scg, d_scg), (s_scu, d_scu), (s_scd, d_scd)):
            ins = nc.sync.dma_start(dst[:], src.ap())
            p.inc(ins, "one", 16)
        nc.sync.wait_ge(p.sems["one"], p.c("one"))
        nc.all_engine_barrier()

        with ExitStack() as att:
            sba = lambda n, s, d: att.enter_context(nc.sbuf_tensor(n, s, d))
            s_kt = sba("s_kt", [128, W], BF16)
            s_v = sba("s_v", [128, W], BF16)
            s_qt = sba("s_qt", [128, NHEADS * R], BF16)

            with ExitStack() as ph:
                sbp = lambda n, s, d: ph.enter_context(nc.sbuf_tensor(n, s, d))
                s_xlo = sbp("s_xlo", [128, HC * R], BF16)
                s_wq = sbp("s_wq", [128, 2 * HC * 128], BF16)
                s_wkv = sbp("s_wkv", [128, HC * 256], BF16)
                s_xf = sbp("s_xf", [128, 2 * 512], F32)
                s_xb = sbp("s_xb", [128, 2 * 512], BF16)
                s_xq = sbp("s_xq", [128, 2 * 512], BF16)
                s_xc = sbp("s_xc", [128, 4 * 512], BF16)
                s_tn = sbp("s_tn", [128, 512], F32)

                ins = nc.gpsimd.dma_start(s_wkv[:], d_wkv.ap())
                wkv_rdy = p.inc(ins, "oneg", 16)
                psp = lambda n, s: ph.enter_context(nc.psum_tensor(n, s, F32))
                P_s1 = psp("p_s1", [128, 512])
                P_s2 = psp("p_s2", [128, 512])
                P_br = psp("p_br", [128, 512])
                P_bc = psp("p_bc", [128, 512])
                p_kt = psp("p_kt", [128, 512])
                p_v = psp("p_v", [128, 512])
                P_q = psp("p_q", [128, 1024])

                xsrc = d_xkv.ap().rearrange("(c p) n -> p c n", p=128)
                PTSEQ = [6, 7, 0, 1, 2, 3, 4, 5]

                def emit_passA(pt, eidx):
                    cols = slice(pt * 512, pt * 512 + 512)
                    for g4 in range(HC):
                        gpar = g4 % 2
                        xb = s_xb[:, gpar * 512:(gpar + 1) * 512]
                        xq = s_xq[:, gpar * 512:(gpar + 1) * 512]
                        gidx = eidx * HC + g4
                        if gidx >= 2:
                            p.wait(nc.gpsimd, "mm1g", gidx - 1)
                        xbs = "xb0" if gidx % 2 == 0 else "xb1"
                        ins = nc.gpsimd.dma_start(xb, xsrc[:, g4, cols])
                        rdy = p.inc(ins, xbs, 16)
                        nc.scalar.wait_ge(p.sems[xbs], rdy)
                        if gidx >= 2:
                            p.wait(nc.scalar, "mm1g", gidx - 1)
                        ins = nc.scalar.activation(xq, xb, AF.Square)
                        p.inc(ins, "sqg")
                        p.wait(nc.tensor, "sqg", gidx + 1)
                        if g4 == 0 and eidx >= 1:
                            p.wait(nc.tensor, "st1", eidx)
                        hc = g4
                        nc.tensor.matmul(
                            P_s1[0:1, :], s_ones[:], xb,
                            start=(hc == 0), stop=(hc == HC - 1))
                        ins = nc.tensor.matmul(
                            P_s2[0:1, :], s_ones[:], xq,
                            start=(hc == 0), stop=(hc == HC - 1))
                        p.inc(ins, "mm1g")

                def emit_rows(pt, eidx):
                    par = eidx % 2
                    r_m, r_rs, r_c = row(par * 3), row(1 + par * 3), row(2 + par * 3)
                    r_e2, r_v, r_ln = row(6), row(7), row(6)
                    p.wait(nc.vector, "mm1g", HC * (eidx + 1))
                    nc.vector.tensor_scalar_mul(r_m, P_s1[0:1, :], 1.0 / H)
                    nc.vector.tensor_scalar_mul(r_e2, P_s2[0:1, :], 1.0 / H)
                    nc.vector.tensor_mul(r_v, r_m, r_m)
                    ins = nc.vector.tensor_sub(r_v, r_e2, r_v)
                    p.inc(ins, "st1")
                    p.wait(nc.scalar, "st1", eidx + 1)
                    nc.scalar.activation(r_ln, r_v, AF.Ln, bias=EPS)
                    ins = nc.scalar.activation(r_rs, r_ln, AF.Exp, scale=-0.5)
                    p.inc(ins, "rsd")
                    p.wait(nc.vector, "rsd", eidx + 1)
                    ins = nc.vector.scalar_tensor_tensor(
                        r_c, r_m, -1.0, r_rs, op0=ALU.mult, op1=ALU.mult)
                    p.inc(ins, "st2")
                    p.wait(nc.tensor, "st2", eidx + 1)
                    if eidx >= 1:
                        p.wait(nc.tensor, "dv", eidx * HC)
                    nc.tensor.matmul(P_br[:], s_ones1[:], r_rs,
                                     start=True, stop=True)
                    ins = nc.tensor.matmul(P_bc[:], s_ones1[:], r_c,
                                           start=True, stop=True)
                    p.inc(ins, "bcd")

                def q_pieces():
                    wq_rdy = [0] * NHEADS
                    ins = nc.gpsimd.dma_start(s_wq[:, 0:HC * 128], d_wq.ap()[0])
                    wq_rdy[0] = p.inc(ins, "wt0", 16)
                    for ct in range(NHEADS):
                        par = ct % 2
                        wq = s_wq[:, par * HC * 128:(par + 1) * HC * 128]
                        if ct + 1 < NHEADS:
                            if ct >= 1:
                                p.wait(nc.gpsimd, "qmm", ct)
                            ins = nc.gpsimd.dma_start(
                                s_wq[:, (1 - par) * HC * 128:(2 - par) * HC * 128],
                                d_wq.ap()[ct + 1])
                            wq_rdy[ct + 1] = p.inc(
                                ins, "wt0" if (ct + 1) % 2 == 0 else "wt1", 16)
                        nc.tensor.wait_ge(
                            p.sems["wt0" if ct % 2 == 0 else "wt1"], wq_rdy[ct])
                        if ct == 0:
                            p.wait(nc.tensor, "dv", 2 * HC)
                        else:
                            p.wait(nc.tensor, "qev", ct)
                        for piece in range(8):
                            for hc in range(piece * 4, piece * 4 + 4):
                                for hf in range(2):
                                    ins = nc.tensor.matmul(
                                        P_q[:, hf * 512:(hf + 1) * 512],
                                        wq[:, hc * 128:(hc + 1) * 128],
                                        s_xlo[:, hc * R + hf * 512:hc * R + hf * 512 + 512],
                                        start=(hc == 0), stop=(hc == HC - 1))
                            if piece < 7:
                                yield
                        p.inc(ins, "qmm")
                        p.wait(nc.scalar, "qmm", ct + 1)
                        nc.scalar.activation(
                            s_qt[:, ct * R:ct * R + 512], P_q[:, 0:512],
                            AF.Copy, scale=QK_SCALE)
                        ins = nc.scalar.activation(
                            s_qt[:, ct * R + 512:ct * R + 1024], P_q[:, 512:1024],
                            AF.Copy, scale=QK_SCALE)
                        p.inc(ins, "qev")
                        yield

                qgen = q_pieces()
                q_done = [False]

                def pump(n):
                    for _ in range(n):
                        if q_done[0]:
                            return
                        try:
                            next(qgen)
                        except StopIteration:
                            q_done[0] = True
                            return

                def emit_passCD(pt, eidx):
                    own = pt >= 6
                    cols = slice(pt * 512, pt * 512 + 512)
                    nchunk = eidx * HC
                    if eidx == 0:
                        nc.tensor.wait_ge(p.sems["oneg"], wkv_rdy)
                    for g4 in range(HC):
                        gidx = eidx * HC + g4
                        xf = s_xf[:, (gidx % 2) * 512:(gidx % 2 + 1) * 512]
                        if gidx >= 2:
                            p.wait(nc.sync, "dv", gidx - 1)
                        xfs = f"xf{gidx % 2}"
                        ins = nc.sync.dma_start(xf, xsrc[:, g4, cols])
                        rdy = p.inc(ins, xfs, 16)
                        hc = g4
                        ch = nchunk + hc
                        if own:
                            q0 = (pt - 6) * 512
                            xd = s_xlo[:, hc * R + q0:hc * R + q0 + 512]
                        else:
                            xd = s_xc[:, (ch % 4) * 512:(ch % 4 + 1) * 512]
                        nc.vector.wait_ge(p.sems[xfs], rdy)
                        if g4 == 0:
                            p.wait(nc.vector, "bcd", eidx + 1)
                        if not own and ch >= 4:
                            p.wait(nc.vector, "kvc", ch - 3)
                        nc.vector.scalar_tensor_tensor(
                            s_tn[:], xf, 1.0,
                            P_br[:], op0=ALU.mult, op1=ALU.mult)
                        ins = nc.vector.scalar_tensor_tensor(
                            xd, s_tn[:], 1.0, P_bc[:],
                            op0=ALU.mult, op1=ALU.add)
                        p.inc(ins, "dv")
                        p.wait(nc.tensor, "dv", p.c("dv"))
                        if hc == 0 and eidx >= 1:
                            p.wait(nc.tensor, "kvx", eidx)
                        nc.tensor.matmul(
                            p_kt[:], s_wkv[:, hc * 256:hc * 256 + 128],
                            xd, start=(hc == 0), stop=(hc == HC - 1))
                        for sub in range(4):
                            ins = nc.tensor.matmul(
                                p_v[:, sub * 128:(sub + 1) * 128],
                                xd[:, sub * 128:(sub + 1) * 128],
                                s_wkv[:, hc * 256 + 128:hc * 256 + 256],
                                start=(hc == 0 and sub == 0),
                                stop=(hc == HC - 1))
                        p.inc(ins, "kvc")
                        if eidx >= 2:
                            pump(1)
                    p.wait(nc.vector, "kvc", nchunk + HC)
                    nc.vector.tensor_copy(s_kt[:, pt * 512:(pt + 1) * 512], p_kt[:])
                    ins = nc.vector.tensor_copy(s_v[:, pt * 512:(pt + 1) * 512], p_v[:])
                    p.inc(ins, "kvx")

                emit_passA(PTSEQ[0], 0)
                for eidx, pt in enumerate(PTSEQ):
                    emit_rows(pt, eidx)
                    if eidx + 1 < 8:
                        emit_passA(PTSEQ[eidx + 1], eidx + 1)
                    emit_passCD(pt, eidx)
                pump(10000)

                if "kt" in d_taps:
                    p.wait(nc.sync, "kvx", 8)
                    ins = nc.sync.dma_start(d_taps["kt"].ap(), s_kt[:])
                    p.inc(ins, "one", 16)
                if "v" in d_taps:
                    p.wait(nc.sync, "kvx", 8)
                    ins = nc.sync.dma_start(d_taps["v"].ap(), s_v[:])
                    p.inc(ins, "one", 16)
                if "qt" in d_taps:
                    p.wait(nc.sync, "qev", NHEADS)
                    ins = nc.sync.dma_start(d_taps["qt"].ap(), s_qt[:])
                    p.inc(ins, "one", 16)
                p.wait(nc.vector, "kvx", 8)
                drain_dma()
                nc.all_engine_barrier()

            with ExitStack() as p34:
                s_at = p34.enter_context(nc.sbuf_tensor("s_at", [128, NHEADS * R], BF16))

                with ExitStack() as ph3:
                    sb3 = lambda n, s, d: ph3.enter_context(nc.sbuf_tensor(n, s, d))
                    s_bias = sb3("s_bias", [128, 2 * NHEADS * 32], F32)
                    s_dt = sb3("s_dt", [128, 2 * 5 * 512], F32)
                    s_ei = sb3("s_ei", [128, 4 * 512], F32)
                    s_pb = sb3("s_pb", [128, 4 * 512], BF16)
                    s_rc = sb3("s_rc", [128, 2 * 512], F32)
                    psp = lambda n, s: ph3.enter_context(nc.psum_tensor(n, s, F32))
                    PS = [psp(f"p_sc{i}", [128, 512]) for i in range(4)]
                    PC = [psp("p_c0", [128, 512]), psp("p_c1", [128, 512])]
                    PD = [psp("p_d0", [128, 512]), psp("p_d1", [128, 512])]

                    ins = nc.sync.dma_start(s_bias[:], d_bias.ap())
                    bias_rdy = p.inc(ins, "one", 16)
                    dt_rdy = [0] * NHEADS
                    ins = nc.gpsimd.dma_start(
                        s_dt[:, 0:5 * 512].rearrange("p (f n) -> p f n", f=5),
                        d_dt.ap()[0].rearrange("f p n -> p f n"))
                    dt_rdy[0] = p.inc(ins, "dd0", 16)

                    slopes_f = _alibi_slopes(NHEADS)
                    adv0 = p.c("adv")

                    slots = []
                    pairs = []
                    for h in range(NHEADS):
                        for qt in range(2):
                            ns = NSLOT[qt]
                            j0 = _slot_start(slopes_f, h, qt)
                            first = len(slots)
                            for j in range(j0, ns):
                                diag = j - (ns - 4)
                                slots.append(dict(
                                    h=h, qt=qt, j=j, pair=len(pairs),
                                    dt_off=(h % 2) * 5 * 512 +
                                           (0 if diag < 0 else (1 + diag) * 512),
                                    bias_col=qt * (NHEADS * 32) + h * 32 + j,
                                    first_of_head=(qt == 0 and j == j0),
                                ))
                            pairs.append(dict(h=h, qt=qt, first=first,
                                              last=len(slots) - 1,
                                              cpar=len(pairs) % 2))
                    S_total = len(slots)
                    last_to_pair = {pr["last"]: pr for pr in pairs}
                    adv_b4_head = [adv0] * (NHEADS + 1)
                    for pr in pairs:
                        if pr["qt"] == 1:
                            adv_b4_head[pr["h"] + 1] = adv0 + pr["last"] + 1

                    dt_prefetch_due = {}
                    for h in range(1, NHEADS):
                        due = adv_b4_head[h - 1] - adv0 if h >= 2 else 0
                        dt_prefetch_due.setdefault(max(due, 0), []).append(h)

                    def issue_dt(h):
                        dpar2 = h % 2
                        if h >= 2:
                            p.wait(nc.gpsimd, "adv", adv_b4_head[h - 1])
                        ins = nc.gpsimd.dma_start(
                            s_dt[:, dpar2 * 5 * 512:(dpar2 + 1) * 5 * 512]
                            .rearrange("p (f n) -> p f n", f=5),
                            d_dt.ap()[h].rearrange("f p n -> p f n"))
                        dt_rdy[h] = p.inc(ins, "dd0" if h % 2 == 0 else "dd1", 16)

                    pe_bcast_due = {}
                    for pr in pairs:
                        pe_bcast_due.setdefault(min(pr["last"] + 2, S_total - 1),
                                                []).append(pr)
                    pv_at = [0] * len(pairs)
                    for g in range(S_total + 3):
                        if g < S_total:
                            sl = slots[g]
                            if g >= 4:
                                p.wait(nc.tensor, "adv", adv0 + g - 3)
                            ins = nc.tensor.matmul(
                                PS[g % 4][:],
                                s_kt[:, sl["j"] * 128:(sl["j"] + 1) * 128],
                                s_qt[:, sl["h"] * R + sl["qt"] * 512:
                                     sl["h"] * R + sl["qt"] * 512 + 512],
                                start=True, stop=True)
                            p.inc(ins, "sc")
                        k = g - 3
                        if k >= 0:
                            sl = slots[k]
                            pr = pairs[sl["pair"]]
                            cpar = pr["cpar"]
                            p.wait(nc.tensor, "ex", k + 1)
                            if k == pr["first"] and sl["pair"] >= 2:
                                p.wait(nc.tensor, "ett", sl["pair"] - 1)
                                p.wait(nc.tensor, "ecp", sl["pair"] - 1)
                            nc.tensor.matmul(
                                PC[cpar][:],
                                s_v[:, sl["j"] * 128:(sl["j"] + 1) * 128],
                                s_pb[:, (k % 4) * 512:(k % 4 + 1) * 512],
                                start=(k == pr["first"]), stop=(k == pr["last"]))
                            ins = nc.tensor.matmul(
                                PD[cpar][0:1, :], s_ones[:],
                                s_pb[:, (k % 4) * 512:(k % 4 + 1) * 512],
                                start=(k == pr["first"]), stop=(k == pr["last"]))
                            pv_at[sl["pair"]] = p.inc(ins, "pv")
                            for pr2 in pe_bcast_due.get(k, ()):
                                p.wait(nc.tensor, "edv", pairs.index(pr2) + 1)
                                ins = nc.tensor.matmul(
                                    PD[pr2["cpar"]][:], s_ones1[:],
                                    row(pr2["cpar"]), start=True, stop=True)
                                p.inc(ins, "epe")

                    dve_copy_due = {}
                    dve_fin_due = {}
                    for pi, pr in enumerate(pairs):
                        dve_copy_due.setdefault(min(pr["last"] + 1, S_total - 1),
                                                []).append(pi)
                        dve_fin_due.setdefault(min(pr["last"] + 4, S_total - 1),
                                               []).append(pi)

                    def dve_copy(pi):
                        pr = pairs[pi]
                        p.wait(nc.vector, "pv", pv_at[pi])
                        if pi >= 2:
                            p.wait(nc.vector, "epe", pi - 1)
                        ins = nc.vector.tensor_copy(row(pr["cpar"]),
                                                    PD[pr["cpar"]][0:1, :])
                        p.inc(ins, "edv")

                    def dve_fin(pi):
                        pr = pairs[pi]
                        p.wait(nc.vector, "epe", pi + 1)
                        ins = nc.vector.reciprocal(
                            s_rc[:, pr["cpar"] * 512:(pr["cpar"] + 1) * 512],
                            PD[pr["cpar"]][:])
                        p.inc(ins, "ecp")
                        ins = nc.vector.scalar_tensor_tensor(
                            s_at[:, pr["h"] * R + pr["qt"] * 512:
                                 pr["h"] * R + pr["qt"] * 512 + 512],
                            PC[pr["cpar"]][:], 1.0,
                            s_rc[:, pr["cpar"] * 512:(pr["cpar"] + 1) * 512],
                            op0=ALU.mult, op1=ALU.mult)
                        p.inc(ins, "ett")

                    for g in range(S_total):
                        sl = slots[g]
                        for h2 in dt_prefetch_due.get(g, ()):
                            issue_dt(h2)
                        if sl["first_of_head"]:
                            nc.vector.wait_ge(
                                p.sems["dd0" if sl["h"] % 2 == 0 else "dd1"],
                                dt_rdy[sl["h"]])
                            if sl["h"] == 0:
                                nc.vector.wait_ge(p.sems["one"], bias_rdy)
                        p.wait(nc.vector, "sc", g + 1)
                        if g >= 4:
                            p.wait(nc.vector, "ex", g - 3)
                        ins = nc.vector.scalar_tensor_tensor(
                            s_ei[:, (g % 4) * 512:(g % 4 + 1) * 512],
                            PS[g % 4][:],
                            s_bias[:, sl["bias_col"]:sl["bias_col"] + 1],
                            s_dt[:, sl["dt_off"]:sl["dt_off"] + 512],
                            op0=ALU.add, op1=ALU.add)
                        p.inc(ins, "adv")
                        for pi in dve_copy_due.get(g, ()):
                            dve_copy(pi)
                        for pi in dve_fin_due.get(g, ()):
                            dve_fin(pi)

                    for g in range(S_total):
                        p.wait(nc.scalar, "adv", adv0 + g + 1)
                        if g >= 4:
                            p.wait(nc.scalar, "pv", g - 3)
                        ins = nc.scalar.activation(
                            s_pb[:, (g % 4) * 512:(g % 4 + 1) * 512],
                            s_ei[:, (g % 4) * 512:(g % 4 + 1) * 512], AF.Exp)
                        p.inc(ins, "ex")

                    if "at" in d_taps:
                        p.wait(nc.sync, "ett", len(pairs))
                        ins = nc.sync.dma_start(d_taps["at"].ap(), s_at[:])
                        p.inc(ins, "one", 16)
                    drain_dma()
                    nc.all_engine_barrier()

                with ExitStack() as ph4:
                    sb4 = lambda n, s, d: ph4.enter_context(nc.sbuf_tensor(n, s, d))
                    ps4 = lambda n, s: ph4.enter_context(nc.psum_tensor(n, s, F32))
                    s_wo = sb4("s_wo", [128, 2 * HC * 128], BF16)
                    s_xr = sb4("s_xr", [128, 2 * R], F32)
                    s_x1 = sb4("s_x1", [128, 2 * R], F32)
                    s_xb16 = sb4("s_xb16", [128, 2 * R], BF16)
                    s_xsq = sb4("s_xsq", [128, 2 * R], BF16)
                    P_o = [ps4(f"p_o{i}", [128, 512]) for i in range(4)]
                    P_s1 = ps4("p4_s1", [128, 1024])
                    P_s2 = ps4("p4_s2", [128, 1024])
                    wo_rdy = [0] * OT_D
                    xr_rdy = [0] * OT_D
                    x1w_at = [0] * OT_D
                    ins = nc.gpsimd.dma_start(s_wo[:, 0:HC * 128], d_wo.ap()[0])
                    wo_rdy[0] = p.inc(ins, "wt0", 16)
                    for ot in range(OT_D):
                        par = ot % 2
                        wo = s_wo[:, par * HC * 128:(par + 1) * HC * 128]
                        if ot >= 2:
                            p.wait(nc.sync, "oev", ot - 1)
                        ins = nc.sync.dma_start(
                            s_xr[:, par * R:(par + 1) * R],
                            d_xkv.ap()[ot * 128:(ot + 1) * 128, QBASE:QBASE + R])
                        xr_rdy[ot] = p.inc(ins, "xf0" if par == 0 else "xf1", 16)
                        if ot + 1 < OT_D:
                            if ot >= 1:
                                p.wait(nc.gpsimd, "omm", ot)
                            ins = nc.gpsimd.dma_start(
                                s_wo[:, (1 - par) * HC * 128:(2 - par) * HC * 128],
                                d_wo.ap()[ot + 1])
                            wo_rdy[ot + 1] = p.inc(ins, "wt0" if (ot + 1) % 2 == 0 else "wt1", 16)
                        nc.tensor.wait_ge(p.sems["wt0" if par == 0 else "wt1"], wo_rdy[ot])
                        if ot >= 2:
                            p.wait(nc.tensor, "oev", ot - 1)
                        for hc in range(HC):
                            for hf in range(2):
                                ins = nc.tensor.matmul(
                                    P_o[2 * par + hf][:],
                                    wo[:, hc * 128:(hc + 1) * 128],
                                    s_at[:, hc * R + hf * 512:hc * R + hf * 512 + 512],
                                    start=(hc == 0), stop=(hc == HC - 1))
                        p.inc(ins, "omm")
                        p.wait(nc.vector, "omm", ot + 1)
                        nc.vector.wait_ge(p.sems["xf0" if par == 0 else "xf1"], xr_rdy[ot])
                        if ot >= 2:
                            nc.vector.wait_ge(p.sems["st0" if par == 0 else "st1x"],
                                              x1w_at[ot - 2])
                            p.wait(nc.vector, "q4s", ot - 1)
                        for hf in range(2):
                            ins = nc.vector.scalar_tensor_tensor(
                                s_x1[:, par * R + hf * 512:par * R + hf * 512 + 512],
                                P_o[2 * par + hf][:], 1.0,
                                s_xr[:, par * R + hf * 512:par * R + hf * 512 + 512],
                                op0=ALU.mult, op1=ALU.add)
                        p.inc(ins, "oev")
                        if ot >= 2:
                            p.wait(nc.vector, "m5", ot - 1)
                        ins = nc.vector.tensor_copy(
                            s_xb16[:, par * R:(par + 1) * R],
                            s_x1[:, par * R:(par + 1) * R])
                        p.inc(ins, "q5")
                        p.wait(nc.scalar, "oev", ot + 1)
                        if ot >= 2:
                            p.wait(nc.scalar, "m5", ot - 1)
                        ins = nc.scalar.activation(
                            s_xsq[:, par * R:(par + 1) * R],
                            s_x1[:, par * R:(par + 1) * R], AF.Square)
                        p.inc(ins, "q4s")
                        p.wait(nc.tensor, "q5", ot + 1)
                        p.wait(nc.tensor, "q4s", ot + 1)
                        for hf in range(2):
                            nc.tensor.matmul(
                                P_s1[0:1, hf * 512:(hf + 1) * 512], s_ones[:],
                                s_xb16[:, par * R + hf * 512:par * R + hf * 512 + 512],
                                start=(ot == 0), stop=(ot == OT_D - 1))
                            ins = nc.tensor.matmul(
                                P_s2[0:1, hf * 512:(hf + 1) * 512], s_ones[:],
                                s_xsq[:, par * R + hf * 512:par * R + hf * 512 + 512],
                                start=(ot == 0), stop=(ot == OT_D - 1))
                        p.inc(ins, "m5")
                        p.wait(nc.gpsimd, "oev", ot + 1)
                        ins = nc.gpsimd.dma_start(d_x1.ap()[ot],
                                                  s_x1[:, par * R:(par + 1) * R])
                        x1w_at[ot] = p.inc(ins, "st0" if par == 0 else "st1x", 16)
                    p.wait(nc.vector, "m5", OT_D)
                    for hf in range(2):
                        r_rs, r_c = row(hf * 2), row(1 + hf * 2)
                        r_m, r_e2, r_v, r_ln = row(4), row(6), row(7), row(6)
                        nc.vector.tensor_scalar_mul(
                            r_m, P_s1[0:1, hf * 512:(hf + 1) * 512], 1.0 / H)
                        nc.vector.tensor_scalar_mul(
                            r_e2, P_s2[0:1, hf * 512:(hf + 1) * 512], 1.0 / H)
                        nc.vector.tensor_mul(r_v, r_m, r_m)
                        ins = nc.vector.tensor_sub(r_v, r_e2, r_v)
                        p.inc(ins, "r5a")
                        p.wait(nc.scalar, "r5a", hf + 1)
                        nc.scalar.activation(r_ln, r_v, AF.Ln, bias=EPS)
                        ins = nc.scalar.activation(r_rs, r_ln, AF.Exp, scale=-0.5)
                        p.inc(ins, "r5b")
                        p.wait(nc.vector, "r5b", hf + 1)
                        ins = nc.vector.scalar_tensor_tensor(
                            r_c, r_m, -1.0, r_rs, op0=ALU.mult, op1=ALU.mult)
                        p.inc(ins, "r5c")
                    drain_dma()
                    nc.all_engine_barrier()

        with ExitStack() as p5:
            s_xl1 = p5.enter_context(nc.sbuf_tensor("s_xl1", [128, HC * R], BF16))

            with ExitStack() as phb:
                sbb = lambda n, s, d: phb.enter_context(nc.sbuf_tensor(n, s, d))
                psb = lambda n, s: phb.enter_context(nc.psum_tensor(n, s, F32))
                s_gw = sbb("s_gw", [128, 2 * HC * 128], BF16)
                s_uw = sbb("s_uw", [128, 2 * HC * 128], BF16)
                s_sg = sbb("s_sg", [128, 2 * R], F32)
                s_su = sbb("s_su", [128, 2 * R], F32)
                s_ht = sbb("s_ht", [128, 2 * R], BF16)
                P_g = [psb(f"p_g{i}", [128, 512]) for i in range(4)]
                P_u = [psb(f"p_u{i}", [128, 512]) for i in range(4)]
                s_x1f = sbb("s_x1f", [128, 4 * R], F32)
                s_tn = sbb("s_tn2", [128, 512], F32)
                s_bb = sbb("s_bb", [128, 4 * 512], F32)
                for hf in range(2):
                    nc.tensor.matmul(P_g[hf][:], s_ones1[:], row(hf * 2),
                                     start=True, stop=True)
                    ins = nc.tensor.matmul(P_g[2 + hf][:], s_ones1[:],
                                           row(1 + hf * 2), start=True, stop=True)
                    p.inc(ins, "b5")
                p.wait(nc.vector, "b5", 2)
                for i in range(4):
                    ins = nc.vector.tensor_copy(s_bb[:, i * 512:(i + 1) * 512],
                                                P_g[i][:])
                p.inc(ins, "r5a")
                b5e_at = p.c("r5a")

                for hc in range(HC):
                    par4 = hc % 4
                    if hc >= 4:
                        p.wait(nc.sync, "n5", hc - 3)
                    xfs = f"xf{hc % 4}"
                    ins = nc.sync.dma_start(s_x1f[:, par4 * R:(par4 + 1) * R],
                                            d_x1.ap()[hc])
                    rdy = p.inc(ins, xfs, 16)
                    nc.vector.wait_ge(p.sems[xfs], rdy)
                    for hf in range(2):
                        nc.vector.scalar_tensor_tensor(
                            s_tn[:],
                            s_x1f[:, par4 * R + hf * 512:par4 * R + hf * 512 + 512],
                            1.0, s_bb[:, hf * 512:(hf + 1) * 512],
                            op0=ALU.mult, op1=ALU.mult)
                        ins = nc.vector.scalar_tensor_tensor(
                            s_xl1[:, hc * R + hf * 512:hc * R + hf * 512 + 512],
                            s_tn[:], 1.0, s_bb[:, (2 + hf) * 512:(3 + hf) * 512],
                            op0=ALU.mult, op1=ALU.add)
                    p.inc(ins, "n5")
                if "xl1" in d_taps:
                    p.wait(nc.sync, "n5", HC)
                    ins = nc.sync.dma_start(d_taps["xl1"].ap(), s_xl1[:])
                    p.inc(ins, "one", 16)
                if "rows" in d_taps:
                    ins = nc.sync.dma_start(d_taps["rows"].ap(), s_rows[:])
                    p.inc(ins, "one", 16)
                gw_rdy = [0] * OT_G
                htw_at = [0] * OT_G
                nc.gpsimd.dma_start(s_gw[:, 0:HC * 128], d_gw.ap()[0]) \
                    .then_inc(p.sems["wt0"], 16)
                p.cnt["wt0"] += 16
                ins = nc.gpsimd.dma_start(s_uw[:, 0:HC * 128], d_uw.ap()[0])
                gw_rdy[0] = p.inc(ins, "wt0", 16)
                for ot in range(OT_G):
                    par = ot % 2
                    gw = s_gw[:, par * HC * 128:(par + 1) * HC * 128]
                    uw = s_uw[:, par * HC * 128:(par + 1) * HC * 128]
                    if ot + 1 < OT_G:
                        if ot >= 1:
                            p.wait(nc.gpsimd, "mmg", ot)
                        nwt = "wt0" if (ot + 1) % 2 == 0 else "wt1"
                        nc.gpsimd.dma_start(
                            s_gw[:, (1 - par) * HC * 128:(2 - par) * HC * 128],
                            d_gw.ap()[ot + 1]).then_inc(p.sems[nwt], 16)
                        p.cnt[nwt] += 16
                        ins = nc.gpsimd.dma_start(
                            s_uw[:, (1 - par) * HC * 128:(2 - par) * HC * 128],
                            d_uw.ap()[ot + 1])
                        gw_rdy[ot + 1] = p.inc(ins, nwt, 16)
                    nc.tensor.wait_ge(p.sems["wt0" if par == 0 else "wt1"], gw_rdy[ot])
                    if ot >= 2:
                        p.wait(nc.tensor, "evg", ot - 1)
                    if ot == 0:
                        p.wait(nc.tensor, "r5a", b5e_at)
                    for hc in range(HC):
                        if ot == 0:
                            p.wait(nc.tensor, "n5", hc + 1)
                        elif ot == 1 and hc == 0:
                            p.wait(nc.tensor, "n5", HC)
                        for hf in range(2):
                            nc.tensor.matmul(
                                P_g[2 * par + hf][:],
                                gw[:, hc * 128:(hc + 1) * 128],
                                s_xl1[:, hc * R + hf * 512:hc * R + hf * 512 + 512],
                                start=(hc == 0), stop=(hc == HC - 1))
                    if ot == 0:
                        p.wait(nc.tensor, "n5", HC)
                    for hc in range(HC):
                        for hf in range(2):
                            ins = nc.tensor.matmul(
                                P_u[2 * par + hf][:],
                                uw[:, hc * 128:(hc + 1) * 128],
                                s_xl1[:, hc * R + hf * 512:hc * R + hf * 512 + 512],
                                start=(hc == 0), stop=(hc == HC - 1))
                    p.inc(ins, "mmg")
                    p.wait(nc.scalar, "mmg", ot + 1)
                    if ot >= 2:
                        p.wait(nc.scalar, "hdv", ot - 1)
                    for hf in range(2):
                        nc.scalar.activation(
                            s_sg[:, par * R + hf * 512:par * R + hf * 512 + 512],
                            P_g[2 * par + hf][:], AF.Silu,
                            scale=s_scg[:, ot:ot + 1])
                        ins = nc.scalar.activation(
                            s_su[:, par * R + hf * 512:par * R + hf * 512 + 512],
                            P_u[2 * par + hf][:], AF.Copy,
                            scale=s_scu[:, ot:ot + 1])
                    p.inc(ins, "evg")
                    p.wait(nc.vector, "evg", ot + 1)
                    if ot >= 2:
                        nc.vector.wait_ge(p.sems["st0" if par == 0 else "st1x"],
                                          htw_at[ot - 2])
                    ins = nc.vector.tensor_mul(s_ht[:, par * R:(par + 1) * R],
                                               s_sg[:, par * R:(par + 1) * R],
                                               s_su[:, par * R:(par + 1) * R])
                    p.inc(ins, "hdv")
                    p.wait(nc.gpsimd, "hdv", ot + 1)
                    sts = "st0" if par == 0 else "st1x"
                    for hf in range(2):
                        ins = nc.gpsimd.dma_start(
                            d_hs[hf].ap()[:, ot * 512:(ot + 1) * 512],
                            s_ht[:, par * R + hf * 512:par * R + hf * 512 + 512])
                        htw_at[ot] = p.inc(ins, sts, 16)
                drain_dma()
                nc.all_engine_barrier()

        with ExitStack() as ph:
            sbp = lambda n, s, d: ph.enter_context(nc.sbuf_tensor(n, s, d))
            psp = lambda n, s: ph.enter_context(nc.psum_tensor(n, s, F32))
            s_hf = sbp("s_hf", [128, OT_G * 512], BF16)
            s_dwb = sbp("s_dwb", [128, 2 * OT_G * 128], BF16)
            s_x1r = sbp("s_x1r", [128, 2 * 512], F32)
            s_ev = sbp("s_ev", [128, 2 * 512], F32)
            s_yo = sbp("s_yo", [128, 2 * 512], F32)
            P_y = [psp("p_y0", [128, 512]), psp("p_y1", [128, 512])]
            outw_at = [0] * (2 * OT_D)
            for half in range(2):
                p.wait(nc.sync, "mmd", p.c("mmd"))
                ins = nc.sync.dma_start(s_hf[:], d_hs[half].ap())
                hf_rdy = p.inc(ins, "one", 16)
                if half == 0:
                    p.wait(nc.gpsimd, "mmd", p.c("mmd"))
                    ins = nc.gpsimd.dma_start(s_dwb[:, 0:OT_G * 128], d_dw.ap()[0])
                    dw_rdy = p.inc(ins, "wt0", 16)
                    dw_sem = "wt0"
                ots = list(range(OT_D)) if half == 0 else list(range(OT_D - 1, -1, -1))
                for oi, ot in enumerate(ots):
                    m = half * OT_D + oi
                    par = ot % 2
                    dw = s_dwb[:, par * OT_G * 128:(par + 1) * OT_G * 128]
                    ndw_rdy = dw_rdy
                    ndw_sem = dw_sem
                    if oi + 1 < OT_D:
                        nxt = ots[oi + 1]
                        p.wait(nc.gpsimd, "mmd", m)
                        ndw_sem = "wt0" if nxt % 2 == 0 else "wt1"
                        ins = nc.gpsimd.dma_start(
                            s_dwb[:, (1 - par) * OT_G * 128:(2 - par) * OT_G * 128],
                            d_dw.ap()[nxt])
                        ndw_rdy = p.inc(ins, ndw_sem, 16)
                    if m >= 2:
                        p.wait(nc.sync, "ydv", m - 1)
                    xfs = "xf0" if par == 0 else "xf1"
                    ins = nc.sync.dma_start(
                        s_x1r[:, par * 512:(par + 1) * 512],
                        d_x1.ap()[ot][:, half * 512:half * 512 + 512])
                    x1r_rdy = p.inc(ins, xfs, 16)
                    nc.tensor.wait_ge(p.sems[dw_sem], dw_rdy)
                    nc.tensor.wait_ge(p.sems["one"], hf_rdy)
                    if m >= 2:
                        p.wait(nc.tensor, "evd", m - 1)
                    for ic in range(OT_G):
                        ins = nc.tensor.matmul(
                            P_y[par][:], dw[:, ic * 128:(ic + 1) * 128],
                            s_hf[:, ic * 512:(ic + 1) * 512],
                            start=(ic == 0), stop=(ic == OT_G - 1))
                    p.inc(ins, "mmd")
                    dw_rdy = ndw_rdy
                    dw_sem = ndw_sem
                    p.wait(nc.scalar, "mmd", m + 1)
                    if m >= 2:
                        p.wait(nc.scalar, "ydv", m - 1)
                    ins = nc.scalar.activation(
                        s_ev[:, par * 512:(par + 1) * 512], P_y[par][:],
                        AF.Copy, scale=s_scd[:, ot:ot + 1])
                    p.inc(ins, "evd")
                    p.wait(nc.vector, "evd", m + 1)
                    nc.vector.wait_ge(p.sems[xfs], x1r_rdy)
                    if m >= 2:
                        nc.vector.wait_ge(p.sems["st0" if par == 0 else "st1x"],
                                          outw_at[m - 2])
                    ins = nc.vector.scalar_tensor_tensor(
                        s_yo[:, par * 512:(par + 1) * 512],
                        s_ev[:, par * 512:(par + 1) * 512], 1.0,
                        s_x1r[:, par * 512:(par + 1) * 512],
                        op0=ALU.mult, op1=ALU.add)
                    p.inc(ins, "ydv")
                    p.wait(nc.gpsimd, "ydv", m + 1)
                    ins = nc.gpsimd.dma_start(
                        d_out.ap()[ot][:, half * 512:half * 512 + 512],
                        s_yo[:, par * 512:(par + 1) * 512])
                    outw_at[m] = p.inc(ins, "st0" if par == 0 else "st1x", 16)
            drain_dma()

    return nc



def _i8view(w_int32):
    w = np.ascontiguousarray(np.asarray(w_int32))
    return w.view(np.int8)[:, 0::4]


def _tile_lhsT(w):
    K, M = w.shape
    t = w.reshape(K // 128, 128, M // 128, 128)
    return np.ascontiguousarray(t.transpose(2, 1, 0, 3)).reshape(M // 128, 128, K)


def _prep_shared(wqkv, wo, gw8, uw8, dw8, gs, us, ds):
    slopes = _alibi_slopes(NHEADS)
    wq_t = _tile_lhsT(wqkv[:, :H])
    wkv_t = np.ascontiguousarray(
        wqkv[:, H:].reshape(HC, 128, 256).transpose(1, 0, 2)).reshape(128, HC * 256)
    wo_t = _tile_lhsT(wo)
    gw_t = _tile_lhsT(np.ascontiguousarray(gw8.T))
    uw_t = _tile_lhsT(np.ascontiguousarray(uw8.T))
    dw_t = _tile_lhsT(np.ascontiguousarray(dw8.T))
    scg_t = np.ascontiguousarray(gs.reshape(OT_G, 128).T)
    scu_t = np.ascontiguousarray(us.reshape(OT_G, 128).T)
    scd_t = np.ascontiguousarray(ds.reshape(OT_D, 128).T)

    dk = np.arange(128, dtype=np.float64)[:, None]
    dq = np.arange(512, dtype=np.float64)[None, :]
    dtiles = np.zeros((NHEADS, 5, 128, 512), np.float32)
    for h in range(NHEADS):
        sl = slopes[h]
        dtiles[h, 0] = (sl * (dk - dq)).astype(np.float32)
        for cls in range(4):
            a = sl * (dk - dq + 128.0 * cls)
            a = np.where((dk - dq + 128.0 * cls) <= 0, a, -60000.0)
            dtiles[h, 1 + cls] = a.astype(np.float32)
    return dict(wqt=wq_t, wkvt=wkv_t, wot=wo_t, gwt=gw_t, uwt=uw_t, dwt=dw_t,
                scg=scg_t, scu=scu_t, scd=scd_t, dtiles=dtiles), slopes


def _core_inputs(x, shared, slopes, c):
    b = c // 4
    r0 = R * (c % 4)
    shift = QBASE - r0
    xT = np.ascontiguousarray(x[b].T)
    xkv = np.zeros((H, W), np.float32)
    xkv[:, shift:] = xT[:, :W - shift]

    biasv = np.zeros((2, NHEADS, 32), np.float64)
    for qt in range(2):
        ns = NSLOT[qt]
        q0c = QBASE + qt * 512
        for h in range(NHEADS):
            for j in range(32):
                if j >= ns:
                    biasv[qt, h, j] = NEGBIG
                elif 128 * j + 128 <= shift:
                    biasv[qt, h, j] = NEGBIG
                elif j >= ns - 4:
                    biasv[qt, h, j] = SHIFT
                else:
                    biasv[qt, h, j] = SHIFT + slopes[h] * (128 * j - q0c)
    bias_in = np.ascontiguousarray(np.broadcast_to(
        biasv.reshape(-1).astype(np.float32)[None, :], (128, 2 * NHEADS * 32)))
    d = dict(shared)
    d["xkv"] = xkv
    d["biasv"] = bias_in
    return d


_CACHE = {}


def kernel(L0_attn_hidden_states, L0_attn_Wqkv, L0_attn_Wo,
           L0_ffn_gate_wq, L0_ffn_gate_scale, L0_ffn_up_wq, L0_ffn_up_scale,
           L0_ffn_down_wq, L0_ffn_down_scale):
    x = np.ascontiguousarray(np.asarray(L0_attn_hidden_states, dtype=np.float32))
    wqkv = np.ascontiguousarray(np.asarray(L0_attn_Wqkv, dtype=np.float32))
    wo = np.ascontiguousarray(np.asarray(L0_attn_Wo, dtype=np.float32))
    gw8 = _i8view(L0_ffn_gate_wq)
    uw8 = _i8view(L0_ffn_up_wq)
    dw8 = _i8view(L0_ffn_down_wq)
    gs = np.asarray(L0_ffn_gate_scale, dtype=np.float32)
    us = np.asarray(L0_ffn_up_scale, dtype=np.float32)
    ds = np.asarray(L0_ffn_down_scale, dtype=np.float32)

    shared, slopes = _prep_shared(wqkv, wo, gw8, uw8, dw8, gs, us, ds)
    in_maps = [_core_inputs(x, shared, slopes, c) for c in range(8)]

    if "nc" not in _CACHE:
        _CACHE["nc"] = build_program(TAPS)
    nc = _CACHE["nc"]

    res = run_bass_kernel_spmd(nc, in_maps, core_ids=list(range(8)))

    out = np.empty((B, S, H), np.float32)
    for c in range(8):
        b = c // 4
        r0 = R * (c % 4)
        yT = np.asarray(res.results[c]["out"], np.float32).reshape(H, R)
        out[b, r0:r0 + R, :] = yT.T
    if TAPS:
        kernel.last_results = res
    return out


if __name__ == "__main__":
    import time
    t0 = time.time()
    nc = build_program()
    print(f"build {time.time()-t0:.1f}s instructions: {nc.n_instructions()}")


# version 12
# speedup vs baseline: 1.0944x; 1.0345x over previous
import math
from contextlib import ExitStack

import numpy as np

import concourse.bass as bass
import concourse.mybir as mybir
from concourse.bass_utils import run_bass_kernel_spmd

F32 = mybir.dt.float32
BF16 = mybir.dt.bfloat16
I8 = mybir.dt.int8
AF = mybir.ActivationFunctionType
ALU = mybir.AluOpType

NHEADS = 32
H = 4096
HD = 128
INTER = 14336
B = 2
S = 4096
R = 1024
W = 4096
QBASE = W - R
NSLOT = (QBASE // 128 + 4, QBASE // 128 + 8)
EPS = 1e-5
QK_SCALE = 1.0 / math.sqrt(HD)
SHIFT = -45.0
NEGBIG = -1e30
HC = 32
OT_G = INTER // 128
OT_D = H // 128

TAPS = ()
ALIBI_CUT = 110.0


def _slot_start(slopes, h, qt):
    ns = NSLOT[qt]
    q0c = QBASE + qt * 512
    for j in range(ns):
        if j >= ns - 4 or slopes[h] * (q0c - 128 * j - 127) < ALIBI_CUT:
            return j
    return ns - 4


def _alibi_slopes(n):
    cp2 = 2 ** math.floor(math.log2(n))
    base = 2.0 ** (-(2.0 ** (-(math.log2(cp2) - 3))))
    slopes = base ** np.arange(1, cp2 + 1, dtype=np.float64)
    if cp2 != n:
        extra_base = 2.0 ** (-(2.0 ** (-(math.log2(2 * cp2) - 3))))
        rem = min(cp2, n - cp2)
        extra = extra_base ** np.arange(1, 1 + 2 * rem, 2, dtype=np.float64)
        slopes = np.concatenate([slopes, extra])
    return slopes




class Ctr:

    def __init__(self, nc, ctx, names):
        self.nc = nc
        self.sems = {n: ctx.enter_context(nc.semaphore(n)) for n in names}
        self.cnt = {n: 0 for n in names}

    def inc(self, ins, name, by=1):
        ins.then_inc(self.sems[name], by)
        self.cnt[name] += by
        return self.cnt[name]

    def c(self, name):
        return self.cnt[name]

    def wait(self, eng, name, val):
        if val > 0:
            eng.wait_ge(self.sems[name], val)


def build_program(taps=()):
    nc = bass.Bass("TRN2", target_bir_lowering=False)

    d_xkv = nc.dram_tensor("xkv", [H, W], F32, kind="ExternalInput")
    d_wq = nc.dram_tensor("wqt", [NHEADS, 128, HC * 128], F32, kind="ExternalInput")
    d_wkv = nc.dram_tensor("wkvt", [128, HC * 256], F32, kind="ExternalInput")
    d_wo = nc.dram_tensor("wot", [OT_D, 128, HC * 128], F32, kind="ExternalInput")
    d_gw = nc.dram_tensor("gwt", [OT_G, 128, HC * 128], I8, kind="ExternalInput")
    d_uw = nc.dram_tensor("uwt", [OT_G, 128, HC * 128], I8, kind="ExternalInput")
    d_dw = nc.dram_tensor("dwt", [OT_D, 128, OT_G * 128], I8, kind="ExternalInput")
    d_scg = nc.dram_tensor("scg", [128, OT_G], F32, kind="ExternalInput")
    d_scu = nc.dram_tensor("scu", [128, OT_G], F32, kind="ExternalInput")
    d_scd = nc.dram_tensor("scd", [128, OT_D], F32, kind="ExternalInput")
    d_dt = nc.dram_tensor("dtiles", [NHEADS, 5, 128, 512], F32, kind="ExternalInput")
    d_bias = nc.dram_tensor("biasv", [128, 2 * NHEADS * 32], F32, kind="ExternalInput")

    d_out = nc.dram_tensor("out", [OT_D, 128, R], F32, kind="ExternalOutput")
    x1_kind = "ExternalOutput" if "x1" in taps else "Internal"
    h_kind = "ExternalOutput" if "h" in taps else "Internal"
    d_x1 = nc.dram_tensor("x1s", [HC, 128, R], F32, kind=x1_kind)
    d_h0 = nc.dram_tensor("h0s", [128, OT_G * 512], BF16, kind=h_kind)
    d_h1 = nc.dram_tensor("h1s", [128, OT_G * 512], BF16, kind=h_kind)
    d_hs = (d_h0, d_h1)
    d_taps = {}
    for t, shape, dt_ in (("kt", [128, W], BF16), ("v", [128, W], BF16),
                          ("qt", [128, NHEADS * R], BF16),
                          ("at", [128, NHEADS * R], BF16),
                          ("xl1", [128, HC * R], BF16),
                          ("rows", [1, 8 * 512], F32)):
        if t in taps:
            d_taps[t] = nc.dram_tensor("tap_" + t, shape, dt_, kind="ExternalOutput")

    names = ["xf0", "xf1", "xf2", "xf3", "st0", "st1x", "one",
             "xb0", "xb1", "wt0", "wt1", "dd0", "dd1", "oneg",
             "sqg", "mm1g", "mm1", "st1", "rsd", "st2", "bcd",
             "dv", "kvc", "kvx", "qmm", "qev",
             "sc", "adv", "ex", "pv", "edv", "epe", "ecp", "ett",
             "omm", "oev",
             "q5", "q4s", "m5", "r5a", "r5b", "r5c", "b5", "n5",
             "mmg", "evg", "hdv", "mmd", "evd", "ydv", "fin"]

    with ExitStack() as top:
        p = Ctr(nc, top, names)
        sb = lambda n, s, d: top.enter_context(nc.sbuf_tensor(n, s, d))

        SY_SEMS = ("xf0", "xf1", "xf2", "xf3", "one")
        GP_SEMS = ("xb0", "xb1", "wt0", "wt1", "dd0", "dd1", "oneg",
                   "st0", "st1x")

        def drain_dma():
            for n_ in SY_SEMS:
                if p.c(n_):
                    nc.sync.wait_ge(p.sems[n_], p.c(n_))
            for n_ in GP_SEMS:
                if p.c(n_):
                    nc.gpsimd.wait_ge(p.sems[n_], p.c(n_))

        s_scg = sb("s_scg", [128, OT_G], F32)
        s_scu = sb("s_scu", [128, OT_G], F32)
        s_scd = sb("s_scd", [128, OT_D], F32)
        s_ones = sb("s_ones", [128, 1], BF16)
        s_ones1 = sb("s_ones1", [1, 128], F32)
        s_rows = sb("s_rows", [1, 8 * 512], F32)
        s_ceps = sb("s_ceps", [128, 1], F32)

        def row(i):
            return s_rows[0:1, i * 512:(i + 1) * 512]

        nc.vector.memset(s_ones[:], 1.0)
        nc.vector.memset(s_ones1[:], 1.0)
        nc.vector.memset(s_ceps[:], EPS)
        nc.const_aps.register(F32, EPS, s_ceps[:])

        for dst, src in ((s_scg, d_scg), (s_scu, d_scu), (s_scd, d_scd)):
            ins = nc.sync.dma_start(dst[:], src.ap())
            p.inc(ins, "one", 16)
        nc.sync.wait_ge(p.sems["one"], p.c("one"))
        nc.all_engine_barrier()

        with ExitStack() as att:
            sba = lambda n, s, d: att.enter_context(nc.sbuf_tensor(n, s, d))
            s_kt = sba("s_kt", [128, W], BF16)
            s_v = sba("s_v", [128, W], BF16)
            s_qt = sba("s_qt", [128, NHEADS * R], BF16)

            with ExitStack() as ph:
                sbp = lambda n, s, d: ph.enter_context(nc.sbuf_tensor(n, s, d))
                s_xlo = sbp("s_xlo", [128, HC * R], BF16)
                s_wq = sbp("s_wq", [128, 2 * HC * 128], BF16)
                s_wkv = sbp("s_wkv", [128, HC * 256], BF16)
                s_xf = sbp("s_xf", [128, 2 * 512], F32)
                s_xb = sbp("s_xb", [128, 2 * 512], BF16)
                s_xq = sbp("s_xq", [128, 2 * 512], BF16)
                s_xc = sbp("s_xc", [128, 4 * 512], BF16)
                s_tn = sbp("s_tn", [128, 512], F32)

                ins = nc.gpsimd.dma_start(s_wkv[:], d_wkv.ap())
                wkv_rdy = p.inc(ins, "oneg", 16)
                psp = lambda n, s: ph.enter_context(nc.psum_tensor(n, s, F32))
                P_s1 = psp("p_s1", [128, 512])
                P_s2 = psp("p_s2", [128, 512])
                P_br = psp("p_br", [128, 512])
                P_bc = psp("p_bc", [128, 512])
                p_kt = psp("p_kt", [128, 512])
                p_v = psp("p_v", [128, 512])
                P_q = psp("p_q", [128, 1024])

                xsrc = d_xkv.ap().rearrange("(c p) n -> p c n", p=128)
                PTSEQ = [6, 7, 0, 1, 2, 3, 4, 5]

                def emit_passA(pt, eidx):
                    cols = slice(pt * 512, pt * 512 + 512)
                    for g4 in range(HC):
                        gpar = g4 % 2
                        xb = s_xb[:, gpar * 512:(gpar + 1) * 512]
                        xq = s_xq[:, gpar * 512:(gpar + 1) * 512]
                        gidx = eidx * HC + g4
                        if gidx >= 2:
                            p.wait(nc.gpsimd, "mm1g", gidx - 1)
                        xbs = "xb0" if gidx % 2 == 0 else "xb1"
                        ins = nc.gpsimd.dma_start(xb, xsrc[:, g4, cols])
                        rdy = p.inc(ins, xbs, 16)
                        nc.scalar.wait_ge(p.sems[xbs], rdy)
                        if gidx >= 2:
                            p.wait(nc.scalar, "mm1g", gidx - 1)
                        ins = nc.scalar.activation(xq, xb, AF.Square)
                        p.inc(ins, "sqg")
                        p.wait(nc.tensor, "sqg", gidx + 1)
                        if g4 == 0 and eidx >= 1:
                            p.wait(nc.tensor, "st1", eidx)
                        hc = g4
                        nc.tensor.matmul(
                            P_s1[0:1, :], s_ones[:], xb,
                            start=(hc == 0), stop=(hc == HC - 1))
                        ins = nc.tensor.matmul(
                            P_s2[0:1, :], s_ones[:], xq,
                            start=(hc == 0), stop=(hc == HC - 1))
                        p.inc(ins, "mm1g")

                def emit_rows(pt, eidx):
                    par = eidx % 2
                    r_m, r_rs, r_c = row(par * 3), row(1 + par * 3), row(2 + par * 3)
                    r_e2, r_v, r_ln = row(6), row(7), row(6)
                    p.wait(nc.vector, "mm1g", HC * (eidx + 1))
                    nc.vector.tensor_scalar_mul(r_m, P_s1[0:1, :], 1.0 / H)
                    nc.vector.tensor_scalar_mul(r_e2, P_s2[0:1, :], 1.0 / H)
                    nc.vector.tensor_mul(r_v, r_m, r_m)
                    ins = nc.vector.tensor_sub(r_v, r_e2, r_v)
                    p.inc(ins, "st1")
                    p.wait(nc.scalar, "st1", eidx + 1)
                    nc.scalar.activation(r_ln, r_v, AF.Ln, bias=EPS)
                    ins = nc.scalar.activation(r_rs, r_ln, AF.Exp, scale=-0.5)
                    p.inc(ins, "rsd")
                    p.wait(nc.vector, "rsd", eidx + 1)
                    ins = nc.vector.scalar_tensor_tensor(
                        r_c, r_m, -1.0, r_rs, op0=ALU.mult, op1=ALU.mult)
                    p.inc(ins, "st2")
                    p.wait(nc.tensor, "st2", eidx + 1)
                    if eidx >= 1:
                        p.wait(nc.tensor, "dv", eidx * HC)
                    nc.tensor.matmul(P_br[:], s_ones1[:], r_rs,
                                     start=True, stop=True)
                    ins = nc.tensor.matmul(P_bc[:], s_ones1[:], r_c,
                                           start=True, stop=True)
                    p.inc(ins, "bcd")

                def q_pieces():
                    wq_rdy = [0] * NHEADS
                    ins = nc.gpsimd.dma_start(s_wq[:, 0:HC * 128], d_wq.ap()[0])
                    wq_rdy[0] = p.inc(ins, "wt0", 16)
                    for ct in range(NHEADS):
                        par = ct % 2
                        wq = s_wq[:, par * HC * 128:(par + 1) * HC * 128]
                        if ct + 1 < NHEADS:
                            if ct >= 1:
                                p.wait(nc.gpsimd, "qmm", ct)
                            ins = nc.gpsimd.dma_start(
                                s_wq[:, (1 - par) * HC * 128:(2 - par) * HC * 128],
                                d_wq.ap()[ct + 1])
                            wq_rdy[ct + 1] = p.inc(
                                ins, "wt0" if (ct + 1) % 2 == 0 else "wt1", 16)
                        nc.tensor.wait_ge(
                            p.sems["wt0" if ct % 2 == 0 else "wt1"], wq_rdy[ct])
                        if ct == 0:
                            p.wait(nc.tensor, "dv", 2 * HC)
                        else:
                            p.wait(nc.tensor, "qev", ct)
                        for piece in range(8):
                            for hc in range(piece * 4, piece * 4 + 4):
                                for hf in range(2):
                                    ins = nc.tensor.matmul(
                                        P_q[:, hf * 512:(hf + 1) * 512],
                                        wq[:, hc * 128:(hc + 1) * 128],
                                        s_xlo[:, hc * R + hf * 512:hc * R + hf * 512 + 512],
                                        start=(hc == 0), stop=(hc == HC - 1))
                            if piece < 7:
                                yield
                        p.inc(ins, "qmm")
                        p.wait(nc.scalar, "qmm", ct + 1)
                        nc.scalar.activation(
                            s_qt[:, ct * R:ct * R + 512], P_q[:, 0:512],
                            AF.Copy, scale=QK_SCALE)
                        ins = nc.scalar.activation(
                            s_qt[:, ct * R + 512:ct * R + 1024], P_q[:, 512:1024],
                            AF.Copy, scale=QK_SCALE)
                        p.inc(ins, "qev")
                        yield

                qgen = q_pieces()
                q_done = [False]

                def pump(n):
                    for _ in range(n):
                        if q_done[0]:
                            return
                        try:
                            next(qgen)
                        except StopIteration:
                            q_done[0] = True
                            return

                def emit_passCD(pt, eidx):
                    own = pt >= 6
                    cols = slice(pt * 512, pt * 512 + 512)
                    nchunk = eidx * HC
                    if eidx == 0:
                        nc.tensor.wait_ge(p.sems["oneg"], wkv_rdy)
                    for g4 in range(HC):
                        gidx = eidx * HC + g4
                        xf = s_xf[:, (gidx % 2) * 512:(gidx % 2 + 1) * 512]
                        if gidx >= 2:
                            p.wait(nc.sync, "dv", gidx - 1)
                        xfs = f"xf{gidx % 2}"
                        ins = nc.sync.dma_start(xf, xsrc[:, g4, cols])
                        rdy = p.inc(ins, xfs, 16)
                        hc = g4
                        ch = nchunk + hc
                        if own:
                            q0 = (pt - 6) * 512
                            xd = s_xlo[:, hc * R + q0:hc * R + q0 + 512]
                        else:
                            xd = s_xc[:, (ch % 4) * 512:(ch % 4 + 1) * 512]
                        nc.vector.wait_ge(p.sems[xfs], rdy)
                        if g4 == 0:
                            p.wait(nc.vector, "bcd", eidx + 1)
                        if not own and ch >= 4:
                            p.wait(nc.vector, "kvc", ch - 3)
                        nc.vector.scalar_tensor_tensor(
                            s_tn[:], xf, 1.0,
                            P_br[:], op0=ALU.mult, op1=ALU.mult)
                        ins = nc.vector.scalar_tensor_tensor(
                            xd, s_tn[:], 1.0, P_bc[:],
                            op0=ALU.mult, op1=ALU.add)
                        p.inc(ins, "dv")
                        p.wait(nc.tensor, "dv", p.c("dv"))
                        if hc == 0 and eidx >= 1:
                            p.wait(nc.tensor, "kvx", eidx)
                        nc.tensor.matmul(
                            p_kt[:], s_wkv[:, hc * 256:hc * 256 + 128],
                            xd, start=(hc == 0), stop=(hc == HC - 1))
                        for sub in range(4):
                            ins = nc.tensor.matmul(
                                p_v[:, sub * 128:(sub + 1) * 128],
                                xd[:, sub * 128:(sub + 1) * 128],
                                s_wkv[:, hc * 256 + 128:hc * 256 + 256],
                                start=(hc == 0 and sub == 0),
                                stop=(hc == HC - 1))
                        p.inc(ins, "kvc")
                        if eidx >= 2:
                            pump(1)
                    p.wait(nc.vector, "kvc", nchunk + HC)
                    nc.vector.tensor_copy(s_kt[:, pt * 512:(pt + 1) * 512], p_kt[:])
                    ins = nc.vector.tensor_copy(s_v[:, pt * 512:(pt + 1) * 512], p_v[:])
                    p.inc(ins, "kvx")

                emit_passA(PTSEQ[0], 0)
                for eidx, pt in enumerate(PTSEQ):
                    emit_rows(pt, eidx)
                    if eidx + 1 < 8:
                        emit_passA(PTSEQ[eidx + 1], eidx + 1)
                    emit_passCD(pt, eidx)
                pump(10000)

                if "kt" in d_taps:
                    p.wait(nc.sync, "kvx", 8)
                    ins = nc.sync.dma_start(d_taps["kt"].ap(), s_kt[:])
                    p.inc(ins, "one", 16)
                if "v" in d_taps:
                    p.wait(nc.sync, "kvx", 8)
                    ins = nc.sync.dma_start(d_taps["v"].ap(), s_v[:])
                    p.inc(ins, "one", 16)
                if "qt" in d_taps:
                    p.wait(nc.sync, "qev", NHEADS)
                    ins = nc.sync.dma_start(d_taps["qt"].ap(), s_qt[:])
                    p.inc(ins, "one", 16)
                p.wait(nc.vector, "kvx", 8)
                drain_dma()
                nc.all_engine_barrier()

            with ExitStack() as p34:
                s_at = p34.enter_context(nc.sbuf_tensor("s_at", [128, NHEADS * R], BF16))

                with ExitStack() as ph3:
                    sb3 = lambda n, s, d: ph3.enter_context(nc.sbuf_tensor(n, s, d))
                    s_bias = sb3("s_bias", [128, 2 * NHEADS * 32], F32)
                    s_dt = sb3("s_dt", [128, 2 * 5 * 512], F32)
                    s_ei = sb3("s_ei", [128, 4 * 512], F32)
                    s_pb = sb3("s_pb", [128, 4 * 512], BF16)
                    s_rc = sb3("s_rc", [128, 2 * 512], F32)
                    psp = lambda n, s: ph3.enter_context(nc.psum_tensor(n, s, F32))
                    PS = [psp(f"p_sc{i}", [128, 512]) for i in range(4)]
                    PC = [psp("p_c0", [128, 512]), psp("p_c1", [128, 512])]
                    PD = [psp("p_d0", [128, 512]), psp("p_d1", [128, 512])]

                    ins = nc.sync.dma_start(s_bias[:], d_bias.ap())
                    bias_rdy = p.inc(ins, "one", 16)
                    dt_rdy = [0] * NHEADS
                    ins = nc.gpsimd.dma_start(
                        s_dt[:, 0:5 * 512].rearrange("p (f n) -> p f n", f=5),
                        d_dt.ap()[0].rearrange("f p n -> p f n"))
                    dt_rdy[0] = p.inc(ins, "dd0", 16)

                    slopes_f = _alibi_slopes(NHEADS)
                    adv0 = p.c("adv")

                    slots = []
                    pairs = []
                    for h in range(NHEADS):
                        for qt in range(2):
                            ns = NSLOT[qt]
                            j0 = _slot_start(slopes_f, h, qt)
                            first = len(slots)
                            for j in range(j0, ns):
                                diag = j - (ns - 4)
                                slots.append(dict(
                                    h=h, qt=qt, j=j, pair=len(pairs),
                                    dt_off=(h % 2) * 5 * 512 +
                                           (0 if diag < 0 else (1 + diag) * 512),
                                    bias_col=qt * (NHEADS * 32) + h * 32 + j,
                                    first_of_head=(qt == 0 and j == j0),
                                ))
                            pairs.append(dict(h=h, qt=qt, first=first,
                                              last=len(slots) - 1,
                                              cpar=len(pairs) % 2))
                    S_total = len(slots)
                    last_to_pair = {pr["last"]: pr for pr in pairs}
                    adv_b4_head = [adv0] * (NHEADS + 1)
                    for pr in pairs:
                        if pr["qt"] == 1:
                            adv_b4_head[pr["h"] + 1] = adv0 + pr["last"] + 1

                    dt_prefetch_due = {}
                    for h in range(1, NHEADS):
                        due = adv_b4_head[h - 1] - adv0 if h >= 2 else 0
                        dt_prefetch_due.setdefault(max(due, 0), []).append(h)

                    def issue_dt(h):
                        dpar2 = h % 2
                        if h >= 2:
                            p.wait(nc.gpsimd, "adv", adv_b4_head[h - 1])
                        ins = nc.gpsimd.dma_start(
                            s_dt[:, dpar2 * 5 * 512:(dpar2 + 1) * 5 * 512]
                            .rearrange("p (f n) -> p f n", f=5),
                            d_dt.ap()[h].rearrange("f p n -> p f n"))
                        dt_rdy[h] = p.inc(ins, "dd0" if h % 2 == 0 else "dd1", 16)

                    pe_bcast_due = {}
                    for pr in pairs:
                        pe_bcast_due.setdefault(min(pr["last"] + 2, S_total - 1),
                                                []).append(pr)
                    pv_at = [0] * len(pairs)
                    for g in range(S_total + 3):
                        if g < S_total:
                            sl = slots[g]
                            if g >= 4:
                                p.wait(nc.tensor, "adv", adv0 + g - 3)
                            ins = nc.tensor.matmul(
                                PS[g % 4][:],
                                s_kt[:, sl["j"] * 128:(sl["j"] + 1) * 128],
                                s_qt[:, sl["h"] * R + sl["qt"] * 512:
                                     sl["h"] * R + sl["qt"] * 512 + 512],
                                start=True, stop=True)
                            p.inc(ins, "sc")
                        k = g - 3
                        if k >= 0:
                            sl = slots[k]
                            pr = pairs[sl["pair"]]
                            cpar = pr["cpar"]
                            p.wait(nc.tensor, "ex", k + 1)
                            if k == pr["first"] and sl["pair"] >= 2:
                                p.wait(nc.tensor, "ett", sl["pair"] - 1)
                                p.wait(nc.tensor, "ecp", sl["pair"] - 1)
                            nc.tensor.matmul(
                                PC[cpar][:],
                                s_v[:, sl["j"] * 128:(sl["j"] + 1) * 128],
                                s_pb[:, (k % 4) * 512:(k % 4 + 1) * 512],
                                start=(k == pr["first"]), stop=(k == pr["last"]))
                            ins = nc.tensor.matmul(
                                PD[cpar][0:1, :], s_ones[:],
                                s_pb[:, (k % 4) * 512:(k % 4 + 1) * 512],
                                start=(k == pr["first"]), stop=(k == pr["last"]))
                            pv_at[sl["pair"]] = p.inc(ins, "pv")
                            for pr2 in pe_bcast_due.get(k, ()):
                                p.wait(nc.tensor, "edv", pairs.index(pr2) + 1)
                                ins = nc.tensor.matmul(
                                    PD[pr2["cpar"]][:], s_ones1[:],
                                    row(pr2["cpar"]), start=True, stop=True)
                                p.inc(ins, "epe")

                    dve_copy_due = {}
                    dve_fin_due = {}
                    for pi, pr in enumerate(pairs):
                        dve_copy_due.setdefault(min(pr["last"] + 1, S_total - 1),
                                                []).append(pi)
                        dve_fin_due.setdefault(min(pr["last"] + 4, S_total - 1),
                                               []).append(pi)

                    def dve_copy(pi):
                        pr = pairs[pi]
                        p.wait(nc.vector, "pv", pv_at[pi])
                        if pi >= 2:
                            p.wait(nc.vector, "epe", pi - 1)
                        ins = nc.vector.tensor_copy(row(pr["cpar"]),
                                                    PD[pr["cpar"]][0:1, :])
                        p.inc(ins, "edv")

                    def dve_fin(pi):
                        pr = pairs[pi]
                        p.wait(nc.vector, "epe", pi + 1)
                        ins = nc.vector.reciprocal(
                            s_rc[:, pr["cpar"] * 512:(pr["cpar"] + 1) * 512],
                            PD[pr["cpar"]][:])
                        p.inc(ins, "ecp")
                        ins = nc.vector.scalar_tensor_tensor(
                            s_at[:, pr["h"] * R + pr["qt"] * 512:
                                 pr["h"] * R + pr["qt"] * 512 + 512],
                            PC[pr["cpar"]][:], 1.0,
                            s_rc[:, pr["cpar"] * 512:(pr["cpar"] + 1) * 512],
                            op0=ALU.mult, op1=ALU.mult)
                        p.inc(ins, "ett")

                    for g in range(S_total):
                        sl = slots[g]
                        for h2 in dt_prefetch_due.get(g, ()):
                            issue_dt(h2)
                        if sl["first_of_head"]:
                            nc.vector.wait_ge(
                                p.sems["dd0" if sl["h"] % 2 == 0 else "dd1"],
                                dt_rdy[sl["h"]])
                            if sl["h"] == 0:
                                nc.vector.wait_ge(p.sems["one"], bias_rdy)
                        p.wait(nc.vector, "sc", g + 1)
                        if g >= 4:
                            p.wait(nc.vector, "ex", g - 3)
                        ins = nc.vector.scalar_tensor_tensor(
                            s_ei[:, (g % 4) * 512:(g % 4 + 1) * 512],
                            PS[g % 4][:],
                            s_bias[:, sl["bias_col"]:sl["bias_col"] + 1],
                            s_dt[:, sl["dt_off"]:sl["dt_off"] + 512],
                            op0=ALU.add, op1=ALU.add)
                        p.inc(ins, "adv")
                        for pi in dve_copy_due.get(g, ()):
                            dve_copy(pi)
                        for pi in dve_fin_due.get(g, ()):
                            dve_fin(pi)

                    for g in range(S_total):
                        p.wait(nc.scalar, "adv", adv0 + g + 1)
                        if g >= 4:
                            p.wait(nc.scalar, "pv", g - 3)
                        ins = nc.scalar.activation(
                            s_pb[:, (g % 4) * 512:(g % 4 + 1) * 512],
                            s_ei[:, (g % 4) * 512:(g % 4 + 1) * 512], AF.Exp)
                        p.inc(ins, "ex")

                    if "at" in d_taps:
                        p.wait(nc.sync, "ett", len(pairs))
                        ins = nc.sync.dma_start(d_taps["at"].ap(), s_at[:])
                        p.inc(ins, "one", 16)
                    drain_dma()
                    nc.all_engine_barrier()

                with ExitStack() as ph4:
                    sb4 = lambda n, s, d: ph4.enter_context(nc.sbuf_tensor(n, s, d))
                    ps4 = lambda n, s: ph4.enter_context(nc.psum_tensor(n, s, F32))
                    s_wo = sb4("s_wo", [128, 2 * HC * 128], BF16)
                    s_xr = sb4("s_xr", [128, 2 * R], F32)
                    s_x1 = sb4("s_x1", [128, 2 * R], F32)
                    s_xb16 = sb4("s_xb16", [128, 2 * R], BF16)
                    s_xsq = sb4("s_xsq", [128, 2 * R], BF16)
                    P_o = [ps4(f"p_o{i}", [128, 512]) for i in range(4)]
                    P_s1 = ps4("p4_s1", [128, 1024])
                    P_s2 = ps4("p4_s2", [128, 1024])
                    wo_rdy = [0] * OT_D
                    xr_rdy = [0] * OT_D
                    x1w_at = [0] * OT_D
                    ins = nc.gpsimd.dma_start(s_wo[:, 0:HC * 128], d_wo.ap()[0])
                    wo_rdy[0] = p.inc(ins, "wt0", 16)
                    for ot in range(OT_D):
                        par = ot % 2
                        wo = s_wo[:, par * HC * 128:(par + 1) * HC * 128]
                        if ot >= 2:
                            p.wait(nc.sync, "oev", ot - 1)
                        ins = nc.sync.dma_start(
                            s_xr[:, par * R:(par + 1) * R],
                            d_xkv.ap()[ot * 128:(ot + 1) * 128, QBASE:QBASE + R])
                        xr_rdy[ot] = p.inc(ins, "xf0" if par == 0 else "xf1", 16)
                        if ot + 1 < OT_D:
                            if ot >= 1:
                                p.wait(nc.gpsimd, "omm", ot)
                            ins = nc.gpsimd.dma_start(
                                s_wo[:, (1 - par) * HC * 128:(2 - par) * HC * 128],
                                d_wo.ap()[ot + 1])
                            wo_rdy[ot + 1] = p.inc(ins, "wt0" if (ot + 1) % 2 == 0 else "wt1", 16)
                        nc.tensor.wait_ge(p.sems["wt0" if par == 0 else "wt1"], wo_rdy[ot])
                        if ot >= 2:
                            p.wait(nc.tensor, "oev", ot - 1)
                        for hc in range(HC):
                            for hf in range(2):
                                ins = nc.tensor.matmul(
                                    P_o[2 * par + hf][:],
                                    wo[:, hc * 128:(hc + 1) * 128],
                                    s_at[:, hc * R + hf * 512:hc * R + hf * 512 + 512],
                                    start=(hc == 0), stop=(hc == HC - 1))
                        p.inc(ins, "omm")
                        p.wait(nc.vector, "omm", ot + 1)
                        nc.vector.wait_ge(p.sems["xf0" if par == 0 else "xf1"], xr_rdy[ot])
                        if ot >= 2:
                            nc.vector.wait_ge(p.sems["st0" if par == 0 else "st1x"],
                                              x1w_at[ot - 2])
                            p.wait(nc.vector, "q4s", ot - 1)
                        for hf in range(2):
                            ins = nc.vector.scalar_tensor_tensor(
                                s_x1[:, par * R + hf * 512:par * R + hf * 512 + 512],
                                P_o[2 * par + hf][:], 1.0,
                                s_xr[:, par * R + hf * 512:par * R + hf * 512 + 512],
                                op0=ALU.mult, op1=ALU.add)
                        p.inc(ins, "oev")
                        if ot >= 2:
                            p.wait(nc.vector, "m5", ot - 1)
                        ins = nc.vector.tensor_copy(
                            s_xb16[:, par * R:(par + 1) * R],
                            s_x1[:, par * R:(par + 1) * R])
                        p.inc(ins, "q5")
                        p.wait(nc.scalar, "oev", ot + 1)
                        if ot >= 2:
                            p.wait(nc.scalar, "m5", ot - 1)
                        ins = nc.scalar.activation(
                            s_xsq[:, par * R:(par + 1) * R],
                            s_x1[:, par * R:(par + 1) * R], AF.Square)
                        p.inc(ins, "q4s")
                        p.wait(nc.tensor, "q5", ot + 1)
                        p.wait(nc.tensor, "q4s", ot + 1)
                        for hf in range(2):
                            nc.tensor.matmul(
                                P_s1[0:1, hf * 512:(hf + 1) * 512], s_ones[:],
                                s_xb16[:, par * R + hf * 512:par * R + hf * 512 + 512],
                                start=(ot == 0), stop=(ot == OT_D - 1))
                            ins = nc.tensor.matmul(
                                P_s2[0:1, hf * 512:(hf + 1) * 512], s_ones[:],
                                s_xsq[:, par * R + hf * 512:par * R + hf * 512 + 512],
                                start=(ot == 0), stop=(ot == OT_D - 1))
                        p.inc(ins, "m5")
                        p.wait(nc.gpsimd, "oev", ot + 1)
                        ins = nc.gpsimd.dma_start(d_x1.ap()[ot],
                                                  s_x1[:, par * R:(par + 1) * R])
                        x1w_at[ot] = p.inc(ins, "st0" if par == 0 else "st1x", 16)
                    p.wait(nc.vector, "m5", OT_D)
                    for hf in range(2):
                        r_rs, r_c = row(hf * 2), row(1 + hf * 2)
                        r_m, r_e2, r_v, r_ln = row(4), row(6), row(7), row(6)
                        nc.vector.tensor_scalar_mul(
                            r_m, P_s1[0:1, hf * 512:(hf + 1) * 512], 1.0 / H)
                        nc.vector.tensor_scalar_mul(
                            r_e2, P_s2[0:1, hf * 512:(hf + 1) * 512], 1.0 / H)
                        nc.vector.tensor_mul(r_v, r_m, r_m)
                        ins = nc.vector.tensor_sub(r_v, r_e2, r_v)
                        p.inc(ins, "r5a")
                        p.wait(nc.scalar, "r5a", hf + 1)
                        nc.scalar.activation(r_ln, r_v, AF.Ln, bias=EPS)
                        ins = nc.scalar.activation(r_rs, r_ln, AF.Exp, scale=-0.5)
                        p.inc(ins, "r5b")
                        p.wait(nc.vector, "r5b", hf + 1)
                        ins = nc.vector.scalar_tensor_tensor(
                            r_c, r_m, -1.0, r_rs, op0=ALU.mult, op1=ALU.mult)
                        p.inc(ins, "r5c")
                    drain_dma()
                    nc.all_engine_barrier()

        with ExitStack() as p5:
            s_xl1 = p5.enter_context(nc.sbuf_tensor("s_xl1", [128, HC * R], BF16))

            with ExitStack() as phb:
                sbb = lambda n, s, d: phb.enter_context(nc.sbuf_tensor(n, s, d))
                psb = lambda n, s: phb.enter_context(nc.psum_tensor(n, s, F32))
                s_gw = sbb("s_gw", [128, 2 * HC * 128], BF16)
                s_uw = sbb("s_uw", [128, 2 * HC * 128], BF16)
                s_sg = sbb("s_sg", [128, 2 * R], F32)
                s_su = sbb("s_su", [128, 2 * R], F32)
                s_ht = sbb("s_ht", [128, 2 * R], BF16)
                P_g = [psb(f"p_g{i}", [128, 512]) for i in range(4)]
                P_u = [psb(f"p_u{i}", [128, 512]) for i in range(4)]
                s_x1f = sbb("s_x1f", [128, 4 * R], F32)
                s_tn = sbb("s_tn2", [128, 512], F32)
                s_bb = sbb("s_bb", [128, 4 * 512], F32)
                for hf in range(2):
                    nc.tensor.matmul(P_g[hf][:], s_ones1[:], row(hf * 2),
                                     start=True, stop=True)
                    ins = nc.tensor.matmul(P_g[2 + hf][:], s_ones1[:],
                                           row(1 + hf * 2), start=True, stop=True)
                    p.inc(ins, "b5")
                p.wait(nc.vector, "b5", 2)
                for i in range(4):
                    ins = nc.vector.tensor_copy(s_bb[:, i * 512:(i + 1) * 512],
                                                P_g[i][:])
                p.inc(ins, "r5a")
                b5e_at = p.c("r5a")

                for hc in range(HC):
                    par4 = hc % 4
                    if hc >= 4:
                        p.wait(nc.sync, "n5", hc - 3)
                    xfs = f"xf{hc % 4}"
                    ins = nc.sync.dma_start(s_x1f[:, par4 * R:(par4 + 1) * R],
                                            d_x1.ap()[hc])
                    rdy = p.inc(ins, xfs, 16)
                    nc.vector.wait_ge(p.sems[xfs], rdy)
                    for hf in range(2):
                        nc.vector.scalar_tensor_tensor(
                            s_tn[:],
                            s_x1f[:, par4 * R + hf * 512:par4 * R + hf * 512 + 512],
                            1.0, s_bb[:, hf * 512:(hf + 1) * 512],
                            op0=ALU.mult, op1=ALU.mult)
                        ins = nc.vector.scalar_tensor_tensor(
                            s_xl1[:, hc * R + hf * 512:hc * R + hf * 512 + 512],
                            s_tn[:], 1.0, s_bb[:, (2 + hf) * 512:(3 + hf) * 512],
                            op0=ALU.mult, op1=ALU.add)
                    p.inc(ins, "n5")
                if "xl1" in d_taps:
                    p.wait(nc.sync, "n5", HC)
                    ins = nc.sync.dma_start(d_taps["xl1"].ap(), s_xl1[:])
                    p.inc(ins, "one", 16)
                if "rows" in d_taps:
                    ins = nc.sync.dma_start(d_taps["rows"].ap(), s_rows[:])
                    p.inc(ins, "one", 16)
                gw_rdy = [0] * OT_G
                htw_at = [0] * OT_G
                nc.gpsimd.dma_start(s_gw[:, 0:HC * 128], d_gw.ap()[0]) \
                    .then_inc(p.sems["wt0"], 16)
                p.cnt["wt0"] += 16
                ins = nc.gpsimd.dma_start(s_uw[:, 0:HC * 128], d_uw.ap()[0])
                gw_rdy[0] = p.inc(ins, "wt0", 16)
                for ot in range(OT_G):
                    par = ot % 2
                    gw = s_gw[:, par * HC * 128:(par + 1) * HC * 128]
                    uw = s_uw[:, par * HC * 128:(par + 1) * HC * 128]
                    if ot + 1 < OT_G:
                        if ot >= 1:
                            p.wait(nc.gpsimd, "mmg", ot)
                        nwt = "wt0" if (ot + 1) % 2 == 0 else "wt1"
                        nc.gpsimd.dma_start(
                            s_gw[:, (1 - par) * HC * 128:(2 - par) * HC * 128],
                            d_gw.ap()[ot + 1]).then_inc(p.sems[nwt], 16)
                        p.cnt[nwt] += 16
                        ins = nc.gpsimd.dma_start(
                            s_uw[:, (1 - par) * HC * 128:(2 - par) * HC * 128],
                            d_uw.ap()[ot + 1])
                        gw_rdy[ot + 1] = p.inc(ins, nwt, 16)
                    nc.tensor.wait_ge(p.sems["wt0" if par == 0 else "wt1"], gw_rdy[ot])
                    if ot >= 2:
                        p.wait(nc.tensor, "evg", ot - 1)
                    if ot == 0:
                        p.wait(nc.tensor, "r5a", b5e_at)
                    for hc in range(HC):
                        if ot == 0:
                            p.wait(nc.tensor, "n5", hc + 1)
                        elif ot == 1 and hc == 0:
                            p.wait(nc.tensor, "n5", HC)
                        for hf in range(2):
                            nc.tensor.matmul(
                                P_g[2 * par + hf][:],
                                gw[:, hc * 128:(hc + 1) * 128],
                                s_xl1[:, hc * R + hf * 512:hc * R + hf * 512 + 512],
                                start=(hc == 0), stop=(hc == HC - 1))
                    if ot == 0:
                        p.wait(nc.tensor, "n5", HC)
                    for hc in range(HC):
                        for hf in range(2):
                            ins = nc.tensor.matmul(
                                P_u[2 * par + hf][:],
                                uw[:, hc * 128:(hc + 1) * 128],
                                s_xl1[:, hc * R + hf * 512:hc * R + hf * 512 + 512],
                                start=(hc == 0), stop=(hc == HC - 1))
                    p.inc(ins, "mmg")
                    p.wait(nc.scalar, "mmg", ot + 1)
                    if ot >= 2:
                        p.wait(nc.scalar, "hdv", ot - 1)
                    for hf in range(2):
                        nc.scalar.activation(
                            s_sg[:, par * R + hf * 512:par * R + hf * 512 + 512],
                            P_g[2 * par + hf][:], AF.Silu,
                            scale=s_scg[:, ot:ot + 1])
                        ins = nc.scalar.activation(
                            s_su[:, par * R + hf * 512:par * R + hf * 512 + 512],
                            P_u[2 * par + hf][:], AF.Copy,
                            scale=s_scu[:, ot:ot + 1])
                    p.inc(ins, "evg")
                    p.wait(nc.vector, "evg", ot + 1)
                    if ot >= 2:
                        nc.vector.wait_ge(p.sems["st0" if par == 0 else "st1x"],
                                          htw_at[ot - 2])
                    ins = nc.vector.tensor_mul(s_ht[:, par * R:(par + 1) * R],
                                               s_sg[:, par * R:(par + 1) * R],
                                               s_su[:, par * R:(par + 1) * R])
                    p.inc(ins, "hdv")
                    p.wait(nc.gpsimd, "hdv", ot + 1)
                    sts = "st0" if par == 0 else "st1x"
                    for hf in range(2):
                        ins = nc.gpsimd.dma_start(
                            d_hs[hf].ap()[:, ot * 512:(ot + 1) * 512],
                            s_ht[:, par * R + hf * 512:par * R + hf * 512 + 512])
                        htw_at[ot] = p.inc(ins, sts, 16)
                drain_dma()
                nc.all_engine_barrier()

        with ExitStack() as ph:
            sbp = lambda n, s, d: ph.enter_context(nc.sbuf_tensor(n, s, d))
            psp = lambda n, s: ph.enter_context(nc.psum_tensor(n, s, F32))
            s_hf = sbp("s_hf", [128, OT_G * 512], BF16)
            s_dwb = sbp("s_dwb", [128, 2 * OT_G * 128], BF16)
            s_x1r = sbp("s_x1r", [128, 2 * 512], F32)
            s_ev = sbp("s_ev", [128, 2 * 512], F32)
            s_yo = sbp("s_yo", [128, 2 * 512], F32)
            P_y = [psp("p_y0", [128, 512]), psp("p_y1", [128, 512])]
            outw_at = [0] * (2 * OT_D)
            for half in range(2):
                p.wait(nc.sync, "mmd", p.c("mmd"))
                p.wait(nc.gpsimd, "mmd", p.c("mmd"))
                p.wait(nc.scalar, "mmd", p.c("mmd"))
                c0, c1 = 38 * 512, 75 * 512
                ins = nc.sync.dma_start(s_hf[:, 0:c0], d_hs[half].ap()[:, 0:c0])
                hf_rdy = p.inc(ins, "one", 16)
                ins = nc.gpsimd.dma_start(s_hf[:, c0:c1], d_hs[half].ap()[:, c0:c1])
                hf_rdy_g = p.inc(ins, "oneg", 16)
                ins = nc.scalar.dma_start(s_hf[:, c1:], d_hs[half].ap()[:, c1:])
                hf_rdy_s = p.inc(ins, "xf2", 16)
                if half == 0:
                    ins = nc.gpsimd.dma_start(s_dwb[:, 0:OT_G * 128], d_dw.ap()[0])
                    dw_rdy = p.inc(ins, "wt0", 16)
                    dw_sem = "wt0"
                ots = list(range(OT_D)) if half == 0 else list(range(OT_D - 1, -1, -1))
                for oi, ot in enumerate(ots):
                    m = half * OT_D + oi
                    par = ot % 2
                    dw = s_dwb[:, par * OT_G * 128:(par + 1) * OT_G * 128]
                    ndw_rdy = dw_rdy
                    ndw_sem = dw_sem
                    if oi + 1 < OT_D:
                        nxt = ots[oi + 1]
                        p.wait(nc.gpsimd, "mmd", m)
                        ndw_sem = "wt0" if nxt % 2 == 0 else "wt1"
                        ins = nc.gpsimd.dma_start(
                            s_dwb[:, (1 - par) * OT_G * 128:(2 - par) * OT_G * 128],
                            d_dw.ap()[nxt])
                        ndw_rdy = p.inc(ins, ndw_sem, 16)
                    if m >= 2:
                        p.wait(nc.sync, "ydv", m - 1)
                    xfs = "xf0" if par == 0 else "xf1"
                    ins = nc.sync.dma_start(
                        s_x1r[:, par * 512:(par + 1) * 512],
                        d_x1.ap()[ot][:, half * 512:half * 512 + 512])
                    x1r_rdy = p.inc(ins, xfs, 16)
                    nc.tensor.wait_ge(p.sems[dw_sem], dw_rdy)
                    nc.tensor.wait_ge(p.sems["one"], hf_rdy)
                    nc.tensor.wait_ge(p.sems["oneg"], hf_rdy_g)
                    nc.tensor.wait_ge(p.sems["xf2"], hf_rdy_s)
                    if m >= 2:
                        p.wait(nc.tensor, "evd", m - 1)
                    for ic in range(OT_G):
                        ins = nc.tensor.matmul(
                            P_y[par][:], dw[:, ic * 128:(ic + 1) * 128],
                            s_hf[:, ic * 512:(ic + 1) * 512],
                            start=(ic == 0), stop=(ic == OT_G - 1))
                    p.inc(ins, "mmd")
                    dw_rdy = ndw_rdy
                    dw_sem = ndw_sem
                    p.wait(nc.scalar, "mmd", m + 1)
                    if m >= 2:
                        p.wait(nc.scalar, "ydv", m - 1)
                    ins = nc.scalar.activation(
                        s_ev[:, par * 512:(par + 1) * 512], P_y[par][:],
                        AF.Copy, scale=s_scd[:, ot:ot + 1])
                    p.inc(ins, "evd")
                    p.wait(nc.vector, "evd", m + 1)
                    nc.vector.wait_ge(p.sems[xfs], x1r_rdy)
                    if m >= 2:
                        nc.vector.wait_ge(p.sems["st0" if par == 0 else "st1x"],
                                          outw_at[m - 2])
                    ins = nc.vector.scalar_tensor_tensor(
                        s_yo[:, par * 512:(par + 1) * 512],
                        s_ev[:, par * 512:(par + 1) * 512], 1.0,
                        s_x1r[:, par * 512:(par + 1) * 512],
                        op0=ALU.mult, op1=ALU.add)
                    p.inc(ins, "ydv")
                    p.wait(nc.gpsimd, "ydv", m + 1)
                    ins = nc.gpsimd.dma_start(
                        d_out.ap()[ot][:, half * 512:half * 512 + 512],
                        s_yo[:, par * 512:(par + 1) * 512])
                    outw_at[m] = p.inc(ins, "st0" if par == 0 else "st1x", 16)
            drain_dma()

    return nc



def _i8view(w_int32):
    w = np.ascontiguousarray(np.asarray(w_int32))
    return w.view(np.int8)[:, 0::4]


def _tile_lhsT(w):
    K, M = w.shape
    t = w.reshape(K // 128, 128, M // 128, 128)
    return np.ascontiguousarray(t.transpose(2, 1, 0, 3)).reshape(M // 128, 128, K)


def _prep_shared(wqkv, wo, gw8, uw8, dw8, gs, us, ds):
    slopes = _alibi_slopes(NHEADS)
    wq_t = _tile_lhsT(wqkv[:, :H])
    wkv_t = np.ascontiguousarray(
        wqkv[:, H:].reshape(HC, 128, 256).transpose(1, 0, 2)).reshape(128, HC * 256)
    wo_t = _tile_lhsT(wo)
    gw_t = _tile_lhsT(np.ascontiguousarray(gw8.T))
    uw_t = _tile_lhsT(np.ascontiguousarray(uw8.T))
    dw_t = _tile_lhsT(np.ascontiguousarray(dw8.T))
    scg_t = np.ascontiguousarray(gs.reshape(OT_G, 128).T)
    scu_t = np.ascontiguousarray(us.reshape(OT_G, 128).T)
    scd_t = np.ascontiguousarray(ds.reshape(OT_D, 128).T)

    dk = np.arange(128, dtype=np.float64)[:, None]
    dq = np.arange(512, dtype=np.float64)[None, :]
    dtiles = np.zeros((NHEADS, 5, 128, 512), np.float32)
    for h in range(NHEADS):
        sl = slopes[h]
        dtiles[h, 0] = (sl * (dk - dq)).astype(np.float32)
        for cls in range(4):
            a = sl * (dk - dq + 128.0 * cls)
            a = np.where((dk - dq + 128.0 * cls) <= 0, a, -60000.0)
            dtiles[h, 1 + cls] = a.astype(np.float32)
    return dict(wqt=wq_t, wkvt=wkv_t, wot=wo_t, gwt=gw_t, uwt=uw_t, dwt=dw_t,
                scg=scg_t, scu=scu_t, scd=scd_t, dtiles=dtiles), slopes


def _core_inputs(x, shared, slopes, c):
    b = c // 4
    r0 = R * (c % 4)
    shift = QBASE - r0
    xT = np.ascontiguousarray(x[b].T)
    xkv = np.zeros((H, W), np.float32)
    xkv[:, shift:] = xT[:, :W - shift]

    biasv = np.zeros((2, NHEADS, 32), np.float64)
    for qt in range(2):
        ns = NSLOT[qt]
        q0c = QBASE + qt * 512
        for h in range(NHEADS):
            for j in range(32):
                if j >= ns:
                    biasv[qt, h, j] = NEGBIG
                elif 128 * j + 128 <= shift:
                    biasv[qt, h, j] = NEGBIG
                elif j >= ns - 4:
                    biasv[qt, h, j] = SHIFT
                else:
                    biasv[qt, h, j] = SHIFT + slopes[h] * (128 * j - q0c)
    bias_in = np.ascontiguousarray(np.broadcast_to(
        biasv.reshape(-1).astype(np.float32)[None, :], (128, 2 * NHEADS * 32)))
    d = dict(shared)
    d["xkv"] = xkv
    d["biasv"] = bias_in
    return d


_CACHE = {}


def kernel(L0_attn_hidden_states, L0_attn_Wqkv, L0_attn_Wo,
           L0_ffn_gate_wq, L0_ffn_gate_scale, L0_ffn_up_wq, L0_ffn_up_scale,
           L0_ffn_down_wq, L0_ffn_down_scale):
    x = np.ascontiguousarray(np.asarray(L0_attn_hidden_states, dtype=np.float32))
    wqkv = np.ascontiguousarray(np.asarray(L0_attn_Wqkv, dtype=np.float32))
    wo = np.ascontiguousarray(np.asarray(L0_attn_Wo, dtype=np.float32))
    gw8 = _i8view(L0_ffn_gate_wq)
    uw8 = _i8view(L0_ffn_up_wq)
    dw8 = _i8view(L0_ffn_down_wq)
    gs = np.asarray(L0_ffn_gate_scale, dtype=np.float32)
    us = np.asarray(L0_ffn_up_scale, dtype=np.float32)
    ds = np.asarray(L0_ffn_down_scale, dtype=np.float32)

    shared, slopes = _prep_shared(wqkv, wo, gw8, uw8, dw8, gs, us, ds)
    in_maps = [_core_inputs(x, shared, slopes, c) for c in range(8)]

    if "nc" not in _CACHE:
        _CACHE["nc"] = build_program(TAPS)
    nc = _CACHE["nc"]

    res = run_bass_kernel_spmd(nc, in_maps, core_ids=list(range(8)))

    out = np.empty((B, S, H), np.float32)
    for c in range(8):
        b = c // 4
        r0 = R * (c % 4)
        yT = np.asarray(res.results[c]["out"], np.float32).reshape(H, R)
        out[b, r0:r0 + R, :] = yT.T
    if TAPS:
        kernel.last_results = res
    return out


if __name__ == "__main__":
    import time
    t0 = time.time()
    nc = build_program()
    print(f"build {time.time()-t0:.1f}s instructions: {nc.n_instructions()}")
